# version 01 (final)
# baseline (speedup 1.0000x reference)
import numpy as np
import ml_dtypes
from contextlib import ExitStack

import concourse.bass as bass
import concourse.mybir as mybir
from concourse.bass_utils import run_bass_kernel_spmd

F32 = mybir.dt.float32
BF16 = mybir.dt.bfloat16
AF = mybir.ActivationFunctionType
ALU = mybir.AluOpType
AX = mybir.AxisListType
BF = ml_dtypes.bfloat16

NCORES = 8
B, S, D = 2, 8192, 1024
TL = 2048
TC = 512
NT = TL // TC
DT = D // 128
C_RG = 8.0
ED, NST, KCONV, RDT = 2048, 16, 4, 64
EDT = ED // 128
HEADS, DH, WIN = 16, 64, 256
INNER = HEADS * DH
FFI, FFP = 2730, 2816
FFT = FFP // 128
NEG = -1e30
SQD = float(np.sqrt(D))
HW = TL + 257



def _np(a):
    return np.asarray(a)


def _cols(v):
    v = _np(v).astype(np.float32).reshape(-1, 128)
    return np.ascontiguousarray(v.T)


def _fm(x2d):
    return np.ascontiguousarray(_np(x2d).T.astype(np.float32))


def _prep(inputs):
    sh = {}
    pc = [dict() for _ in range(NCORES)]
    g = lambda n: _np(inputs[n]).astype(np.float32)

    x = g("L0_attn_x")
    for c in range(NCORES):
        b, q = divmod(c, 4)
        pc[c]["x0"] = _fm(x[b, q * TL:(q + 1) * TL, :])

    for nm in ["W_lin", "W_conv", "W_lin2", "Wa", "Wx", "W1", "W2", "W3"]:
        sh["L0_" + nm] = np.ascontiguousarray(g("L0_attn_" + nm).T)
    for nm in ["b_lin", "b_conv", "b_lin2", "ba", "bx", "b1", "b2", "b3",
               "g1", "g2", "Lam"]:
        sh["L0_" + nm] = _cols(g("L0_attn_" + nm))

    sh["Wg"] = np.ascontiguousarray(g("L0_ffn_Wg"))
    for nm in ["Ws1", "Ws3", "Ws2", "We1", "We3", "We2"]:
        sh[nm] = np.ascontiguousarray(g("L0_ffn_" + nm))

    for l in range(2):
        sh[f"m{l}_inproj"] = np.ascontiguousarray(g("L1_attn_in_proj_w")[l].T)
        sh[f"m{l}_xproj"] = np.ascontiguousarray(g("L1_attn_x_proj_w")[l].T)
        sh[f"m{l}_dtw"] = np.ascontiguousarray(g("L1_attn_dt_w")[l].T)
        sh[f"m{l}_outproj"] = np.ascontiguousarray(g("L1_attn_out_proj_w")[l].T)
        cw = _np(g("L1_attn_conv_w")[l]).reshape(ED, KCONV)
        for k in range(KCONV):
            sh[f"m{l}_cw{k}"] = _cols(cw[:, k])
        sh[f"m{l}_convb"] = _cols(g("L1_attn_conv_b")[l])
        sh[f"m{l}_dtb"] = _cols(g("L1_attn_dt_b")[l])
        al = g("L1_attn_A_log")[l].reshape(EDT, 128, NST)
        sh[f"m{l}_Alog"] = np.ascontiguousarray(
            al.transpose(1, 0, 2).reshape(128, EDT * NST))
        sh[f"m{l}_Dp"] = _cols(g("L1_attn_D_p")[l])
        sh[f"m{l}_normw"] = _cols(g("L1_attn_norm_w")[l])

    for l in range(6):
        sh[f"f{l}_Wqkv"] = np.ascontiguousarray(g("L1_ffn_Wqkv")[l])
        sh[f"f{l}_Wo"] = np.ascontiguousarray(g("L1_ffn_Wo")[l])
        W1 = g("L1_ffn_W1")[l]
        W1a = np.zeros((D, FFP), np.float32); W1a[:, :FFI] = W1[:, :FFI]
        W1g = np.zeros((D, FFP), np.float32); W1g[:, :FFI] = W1[:, FFI:]
        sh[f"f{l}_W1a"], sh[f"f{l}_W1g"] = W1a, W1g
        b1 = g("L1_ffn_b1")[l]
        b1a = np.zeros(FFP, np.float32); b1a[:FFI] = b1[:FFI]
        b1g = np.zeros(FFP, np.float32); b1g[:FFI] = b1[FFI:]
        sh[f"f{l}_b1a"], sh[f"f{l}_b1g"] = _cols(b1a), _cols(b1g)
        W2p = np.zeros((FFP, D), np.float32); W2p[:FFI, :] = g("L1_ffn_W2")[l]
        sh[f"f{l}_W2"] = W2p
        sh[f"f{l}_b2"] = _cols(g("L1_ffn_b2")[l])
        sh[f"f{l}_lng"] = _cols(g("L1_ffn_ln_g")[l])
        sh[f"f{l}_lnb"] = _cols(g("L1_ffn_ln_b")[l])
        sh[f"f{l}_qs"] = np.ascontiguousarray(
            np.tile(g("L1_ffn_q_scale")[l], 2).astype(np.float32).reshape(128, 1))
        sh[f"f{l}_ks"] = np.ascontiguousarray(
            np.tile(g("L1_ffn_k_scale")[l], 2).astype(np.float32).reshape(128, 1))

    sh["id128"] = np.ascontiguousarray(np.eye(128, dtype=np.float32).astype(BF))

    sel16 = np.zeros((16, 16 * 128), np.float32)
    for j in range(16):
        sel16[j, j * 128:(j + 1) * 128] = 1.0
    sh["sel16"] = np.ascontiguousarray(sel16.astype(BF))

    sel32 = np.zeros((32, 32 * 128), np.float32)
    for j in range(32):
        sel32[j, j * 128:(j + 1) * 128] = 1.0
    sh["sel32"] = np.ascontiguousarray(sel32.astype(BF))

    sh["c_b_lin2"] = sh["L0_b_lin2"]
    sh["c_b1"] = sh["L0_b1"]
    sh["c_b2"] = sh["L0_b2"]
    sh["c_b3"] = sh["L0_b3"]
    sh["c_g2"] = sh["L0_g2"]
    sh["c_normw0"] = sh["m0_normw"]

    wsel = np.zeros((8, 8 * 128), np.float32)
    for ex in range(8):
        wsel[ex, ex * 128:(ex + 1) * 128] = 1.0
    sh["wsel"] = wsel

    for c in range(NCORES):
        b, q = divmod(c, 4)
        beta = np.array([1.0 if j < q else 0.0 for j in range(3)], np.float32)
        gam = np.array([1.0 if (0 < j < q) else 0.0 for j in range(3)], np.float32)
        pc[c]["beta"] = np.ascontiguousarray(np.broadcast_to(beta, (128, 3)).copy())
        pc[c]["gamma"] = np.ascontiguousarray(np.broadcast_to(gam, (128, 3)).copy())

    _prep_l1f_tables(pc)
    return sh, pc


def _rotary_tables():
    n = S
    half = np.arange(0, DH, 2, dtype=np.float64)
    inv_freq = 1.0 / 10000.0 ** (half / DH)
    t = np.arange(n, dtype=np.float64)
    freqs = t[:, None] * inv_freq[None, :]
    freqs = np.concatenate([freqs, freqs], -1)
    base_scale = (half + 0.4 * DH) / (1.4 * DH)
    power = (t - n // 2) / (WIN // 2)
    scale = base_scale[None, :] ** power[:, None]
    scale = np.concatenate([scale, scale], -1)
    return freqs, scale


def _prep_l1f_tables(pc):
    freqs, scale = _rotary_tables()
    cos, sin = np.cos(freqs), np.sin(freqs)
    tabs = {"csq": cos * scale, "ssq": sin * scale,
            "csk": cos / scale, "ssk": sin / scale}
    for c in range(NCORES):
        b, q = divmod(c, 4)
        start = q * TL
        poss = np.arange(start - 257, start + TL)
        val = poss >= 0
        posc = np.clip(poss, 0, S - 1)
        for nm, tab in tabs.items():
            tt = tab[posc].T.astype(np.float32)
            tt[:, ~val] = 0.0
            pc[c][nm] = np.ascontiguousarray(np.vstack([tt, tt]))
        masks, koffs = [], []
        for i in range(-1, TL // 128):
            t0 = i * 128
            rows = start + t0 + np.arange(128)
            koff = 0 if i < 0 else max(257 + WIN * ((t0 // WIN) - 1), 0)
            keyg = (start - 257 + koff) + np.arange(512)
            it = rows[:, None].astype(np.int64)
            jb = keyg[None, :].astype(np.int64)
            wq = np.floor_divide(it, WIN)
            inwin = (jb >= (wq - 1) * WIN) & (jb < (wq + 1) * WIN)
            m = (it < jb) | (it > jb + WIN) | (~inwin) | (jb < 0)
            masks.append(np.where(m, NEG, 0.0).astype(np.float32))
            koffs.append(koff)
        pc[c]["attn_masks"] = np.ascontiguousarray(
            np.stack(masks).astype(BF).reshape(17 * 128, 512))
        pc[c]["_koffs"] = np.asarray(koffs, np.int64)
        Pr_ = np.zeros((128, 128), np.float32)
        for blkb in range(2):
            o = blkb * 64
            for d in range(32):
                Pr_[o + d + 32, o + d] = -1.0
                Pr_[o + d, o + d + 32] = 1.0
        pc[c]["Prot"] = Pr_



class Sem:
    __slots__ = ("h", "n")

    def __init__(self, h):
        self.h = h
        self.n = 0


class Pr:
    def __init__(self, name):
        self.name = name
        import os as _os
        self.nc = bass.Bass(
            "TRN2", target_bir_lowering=False, debug=False,
            detect_race_conditions=bool(int(_os.environ.get("K_RACE", "0"))))
        self.ctx = ExitStack()
        self.in_names = []
        self.out_names = []
        self._wm = {}
        self.blk = None

    def dram_in(self, name, shape, dt=F32):
        self.in_names.append(name)
        return self.nc.dram_tensor(name, list(shape), dt, kind="ExternalInput").ap()

    def dram_out(self, name, shape, dt=F32):
        self.out_names.append(name)
        return self.nc.dram_tensor(name, list(shape), dt, kind="ExternalOutput").ap()

    def sbuf(self, name, shape, dt=F32):
        return self.ctx.enter_context(self.nc.sbuf_tensor(name, list(shape), dt))

    def psum(self, name, shape, dt=F32):
        return self.ctx.enter_context(self.nc.psum_tensor(name, list(shape), dt))

    def sem(self, name):
        return Sem(self.ctx.enter_context(self.nc.semaphore(name)))

    def open_block(self):
        self.blk = self.ctx.enter_context(self.nc.Block())
        return self.blk

    def wait(self, eng, ename, sem, val=None):
        v = sem.n if val is None else val
        if v <= 0:
            return
        key = (ename, id(sem))
        if self._wm.get(key, -1) >= v:
            return
        eng.wait_ge(sem.h, v)
        self._wm[key] = v


def inc(inst, sem, k=1):
    inst.then_inc(sem.h, k)
    sem.n += k
    return sem.n


def body(blk, engname):
    return getattr(blk, engname)


def sg(p, e, ename, sem, inst):
    inc(inst, sem)
    p.wait(e, ename, sem, sem.n)


def mm(p, s_t, psums, nk, no, wget, rget, evacs, tdeps=(), edeps=()):
    blk = p.blk
    nb = len(psums)
    base_t = s_t.n
    evacs = [ev if len(ev) == 4 else (*ev, 1) for ev in evacs]
    bases = [s_e.n for (_, s_e, _, _) in evacs]

    @body(blk, "tensor")
    def _(e):
        for (_, s_e, _, _), be in zip(evacs, bases):
            p.wait(e, "tensor", s_e, be)
        for sm, v in tdeps:
            p.wait(e, "tensor", sm, v)
        for o in range(no):
            if o >= nb:
                for (_, s_e, _, ipt), be in zip(evacs, bases):
                    p.wait(e, "tensor", s_e, be + ipt * (o - nb + 1))
            ps = psums[o % nb]
            for k in range(nk):
                i = e.matmul(ps[:, :], wget(k, o), rget(k),
                             start=(k == 0), stop=(k == nk - 1))
            inc(i, s_t)

    for j, (ee, s_e, fn, ipt) in enumerate(evacs):
        prev = evacs[j - 1] if j > 0 else None
        pbase = bases[j - 1] if j > 0 else 0
        @body(blk, ee)
        def _(e, ee=ee, s_e=s_e, fn=fn, ipt=ipt, prev=prev, pbase=pbase):
            for sm, v in edeps:
                p.wait(e, ee, sm, v)
            for o in range(no):
                p.wait(e, ee, s_t, base_t + o + 1)
                if prev is not None:
                    p.wait(e, ee, prev[1], pbase + prev[3] * (o + 1))
                last = fn(e, o, psums[o % nb])
                n_before = s_e.n
                inc(last, s_e)
                assert s_e.n == n_before + 1



def build_L01a(upto=99):
    p = Pr("L01a")
    x0 = p.dram_in("x0", [D, TL])
    Wg_d = p.dram_in("Wg", [D, 8])
    Wmats = {nm: p.dram_in("L0_" + nm, [D, D])
             for nm in ["W_lin", "W_conv", "Wa", "Wx"]}
    cols_d = {nm: p.dram_in("L0_" + nm, [128, DT])
              for nm in ["b_lin", "b_conv", "ba", "bx", "g1", "Lam"]}

    o_xn = p.dram_out("xn", [D, TL])
    o_xgel = p.dram_out("xgel", [D, TL], BF16)
    o_hs = p.dram_out("hs_loc", [D, TL])
    o_P = p.dram_out("P_loc", [D, TL])
    o_wrow = p.dram_out("wrow", [8, TL])
    import os
    _dbg = int(os.environ.get("L01A_DBG", "0"))
    if _dbg:
        o_dba = p.dram_out("dbg_a", [128, DT * TC])
        o_dbg_g = p.dram_out("dbg_gat", [128, DT * TC])
        o_dbg_it = p.dram_out("dbg_it", [128, DT * TC])
        o_dbg_rg = p.dram_out("dbg_rg", [128, DT * TC], BF16)
        o_dbg_lin = p.dram_out("dbg_lin", [128, DT * TC], BF16)
        o_dbg_sq = p.dram_out("dbg_sq", [128, DT * TC])
        o_dbg_si = p.dram_out("dbg_si", [128, DT * TC])
        o_dbg_asq = p.dram_out("dbg_asq", [128, DT * TC])
        o_dbg_lac = p.dram_out("dbg_lac", [128, DT])
        o_dbg_lam = p.dram_out("dbg_lam", [128, DT])
        o_dbg_cs = p.dram_out("dbg_cs", [128, DT])
        o_dbg_cs2 = p.dram_out("dbg_cs2", [128, DT])
        o_dbg_rt = p.dram_out("dbg_rt", [128, DT * TC])

    with p.ctx:
        x_f = p.sbuf("x_f", [128, DT * TC])
        xn_f = x_f
        x_bf = p.sbuf("x_bf", [128, DT * TC], BF16)
        sq_bf = p.sbuf("sq_bf", [128, DT * TC], BF16)
        rows = p.sbuf("rows", [1, 4 * TC])
        ones_b = p.sbuf("ones_b", [128, 1], BF16)
        ones_f1 = p.sbuf("ones_f1", [1, 128])
        ident_f = p.sbuf("ident_f", [128, 128])
        cols = p.sbuf("cols", [128, 6 * DT])
        lac = p.sbuf("lac", [128, DT])
        cscr = p.sbuf("cscr", [128, DT])
        cscr2 = p.sbuf("cscr2", [128, DT])
        lin_bf = p.sbuf("lin_bf", [128, DT * TC], BF16)
        rg_bf = p.sbuf("rg_bf", [128, DT * TC], BF16)
        rg_f = p.sbuf("rg_f", [128, DT * TC])
        rt_f = p.sbuf("rt_f", [128, DT * TC])
        a_f = p.sbuf("a_f", [128, DT * TC])
        gat_f = p.sbuf("gat_f", [128, DT * TC])
        hs_f = rg_f
        Ps_f = rt_f
        carry = p.sbuf("carry", [128, 2 * DT])
        xgel_c = p.sbuf("xgel_c", [128, DT * TC], BF16)
        w4 = {nm: p.sbuf("w_" + nm, [128, DT * D], BF16)
              for nm in ["W_lin", "W_conv", "Wa", "Wx"]}
        wg_f = p.sbuf("wg_f", [128, DT * 8])
        wrow = p.sbuf("wrow_s", [8, TL])
        gt_w = p.sbuf("gt_w", [128, 8])
        gt_t = p.sbuf("gt_t", [128, 48])
        ones8 = p.sbuf("ones8", [128, 8])

        psA = [p.psum(f"psA{i}", [128, TC]) for i in range(3)]
        psH1 = p.psum("psH1", [128, TC])
        psH2 = p.psum("psH2", [128, TC])
        psSt = p.psum("psSt", [1, TC])
        psBc = p.psum("psBc", [128, TC])
        psBc2 = p.psum("psBc2", [128, TC])

        s_ld = p.sem("s_ld")
        s_w = p.sem("s_w")
        s_g = p.sem("s_g")
        s_t = p.sem("s_t")
        s_v = p.sem("s_v")
        s_a = p.sem("s_a")
        s_st = p.sem("s_st")
        blk = p.open_block()

        @body(blk, "sync")
        def _(e):
            for j, nm in enumerate(["b_lin", "b_conv", "ba", "bx", "g1", "Lam"]):
                e.dma_start(out=cols[:, j * DT:(j + 1) * DT],
                            in_=cols_d[nm]).then_inc(s_ld.h, 16)
            s_ld.n += 6 * 16
            e.wait_ge(s_ld.h, s_ld.n)
            e.dma_start(
                out=bass.AP(wg_f, 0, [[DT * 8, 128], [8, DT], [1, 8]]),
                in_=Wg_d.rearrange("(k p) e -> p k e", p=128)
            ).then_inc(s_ld.h, 16)
            s_ld.n += 16
            e.wait_ge(s_ld.h, s_ld.n)

        @body(blk, "gpsimd")
        def _(e):
            for nm in ["W_lin", "W_conv", "Wa", "Wx"]:
                e.dma_start(
                    out=bass.AP(w4[nm], 0, [[DT * D, 128], [D, DT], [1, D]]),
                    in_=Wmats[nm].rearrange("(k p) o -> p k o", p=128)
                ).then_inc(s_w.h, 16)
            s_w.n += 4 * 16
            e.memset(ident_f[:], 1.0)
            i = e.affine_select(ident_f[:], ident_f[:], [[1, 128]],
                                ALU.is_equal, 0.0, base=0, channel_multiplier=-1)
            inc(i, s_g)

        @body(blk, "vector")
        def _(e):
            e.memset(ones_b[:], 1.0)
            e.memset(ones_f1[:], 1.0)
            e.memset(ones8[:], 1.0)
            i = e.memset(carry[:], 0.0)
            inc(i, s_v)

        @body(blk, "scalar")
        def _(e):
            p.wait(e, "scalar", s_ld, s_ld.n)
            lamc = cols[:, 5 * DT:6 * DT]
            sg(p, e, "scalar", s_a, e.activation(cscr2[:], lamc, AF.Abs))
            sg(p, e, "scalar", s_a,
               e.activation(cscr[:], cscr2[:], AF.Exp, scale=-1.0))
            sg(p, e, "scalar", s_a,
               e.activation(cscr2[:], cscr[:], AF.Ln, bias=1.0))
            i = e.activation(lac[:], lamc, AF.Relu, scale=-1.0)
            inc(i, s_a)

        @body(blk, "vector")
        def _(e):
            p.wait(e, "vector", s_a, s_a.n)
            sg(p, e, "vector", s_v,
               e.tensor_tensor(cscr[:], lac[:], cscr2[:], ALU.add))
            i = e.tensor_scalar(lac[:], cscr[:], -1.0 / C_RG, None, ALU.mult)
            inc(i, s_v)

        for tc in range(NT):
            tsl = slice(tc * TC, (tc + 1) * TC)

            @body(blk, "sync")
            def _(e):
                if tc > 0:
                    p.wait(e, "sync", s_v, s_v.n)
                for d in range(DT):
                    e.dma_start(out=x_f[:, d * TC:(d + 1) * TC],
                                in_=x0[d * 128:(d + 1) * 128, tsl]
                                ).then_inc(s_ld.h, 16)
                s_ld.n += DT * 16
                e.wait_ge(s_ld.h, s_ld.n)

            @body(blk, "scalar")
            def _(e):
                p.wait(e, "scalar", s_ld, s_ld.n)
                p.wait(e, "scalar", s_t, s_t.n)
                i = e.activation(sq_bf[:], x_f[:], AF.Square)
                inc(i, s_a)
                i = e.activation(x_bf[:], x_f[:], AF.Copy)
                inc(i, s_a)

            @body(blk, "tensor")
            def _(e):
                p.wait(e, "tensor", s_a, s_a.n)
                p.wait(e, "tensor", s_v, s_v.n)
                for d in range(DT):
                    i = e.matmul(psSt[0:1, :], ones_b[:],
                                 x_bf[:, d * TC:(d + 1) * TC],
                                 start=(d == 0), stop=(d == DT - 1))
                inc(i, s_t)

            @body(blk, "vector")
            def _(e):
                p.wait(e, "vector", s_t, s_t.n)
                i = e.tensor_scalar(rows[0:1, 0:TC], psSt[0:1, :], 1.0 / D,
                                    None, ALU.mult)
                inc(i, s_v)

            @body(blk, "tensor")
            def _(e):
                p.wait(e, "tensor", s_v, s_v.n)
                for d in range(DT):
                    i = e.matmul(psSt[0:1, :], ones_b[:],
                                 sq_bf[:, d * TC:(d + 1) * TC],
                                 start=(d == 0), stop=(d == DT - 1))
                inc(i, s_t)

            @body(blk, "vector")
            def _(e):
                p.wait(e, "vector", s_t, s_t.n)
                r0 = rows[0:1, 0:TC]
                r1 = rows[0:1, TC:2 * TC]
                r2 = rows[0:1, 2 * TC:3 * TC]
                e.tensor_scalar(r1, psSt[0:1, :], 1.0 / D, None, ALU.mult)
                sg(p, e, "vector", s_v,
                   e.tensor_tensor(r2, r0, r0, ALU.mult))
                sg(p, e, "vector", s_v,
                   e.tensor_tensor(r2, r1, r2, ALU.subtract))
                i = e.tensor_scalar(r1, r2, 1e-5, None, ALU.add)
                inc(i, s_v)

            @body(blk, "scalar")
            def _(e):
                p.wait(e, "scalar", s_v, s_v.n)
                r1 = rows[0:1, TC:2 * TC]
                r2 = rows[0:1, 2 * TC:3 * TC]
                sg(p, e, "scalar", s_a, e.activation(r2, r1, AF.Ln))
                i = e.activation(r1, r2, AF.Exp, scale=-0.5)
                inc(i, s_a)

            @body(blk, "vector")
            def _(e):
                p.wait(e, "vector", s_a, s_a.n)
                r0 = rows[0:1, 0:TC]
                r1 = rows[0:1, TC:2 * TC]
                r2 = rows[0:1, 2 * TC:3 * TC]
                r3 = rows[0:1, 3 * TC:4 * TC]
                sg(p, e, "vector", s_v,
                   e.tensor_tensor(r2, r0, r1, ALU.mult))
                i = e.tensor_scalar(r3, r2, -1.0, None, ALU.mult)
                inc(i, s_v)

            @body(blk, "tensor")
            def _(e):
                p.wait(e, "tensor", s_v, s_v.n)
                e.matmul(psBc[:, :], ones_f1[:], rows[0:1, TC:2 * TC],
                         start=True, stop=True)
                i = e.matmul(psBc2[:, :], ones_f1[:], rows[0:1, 3 * TC:4 * TC],
                             start=True, stop=True)
                inc(i, s_t)

            @body(blk, "vector")
            def _(e):
                p.wait(e, "vector", s_t, s_t.n)
                for d in range(DT):
                    dsl = slice(d * TC, (d + 1) * TC)
                    i = e.tensor_tensor(xn_f[:, dsl], x_f[:, dsl], psBc[:, :],
                                        ALU.mult)
                    inc(i, s_v)
                for d in range(DT):
                    dsl = slice(d * TC, (d + 1) * TC)
                    p.wait(e, "vector", s_v, s_v.n)
                    i = e.tensor_tensor(xn_f[:, dsl], xn_f[:, dsl],
                                        psBc2[:, :], ALU.add)
                    inc(i, s_v)

            xnv = s_v.n

            @body(blk, "sync")
            def _(e):
                p.wait(e, "sync", s_v, xnv)
                for d in range(DT):
                    e.dma_start(out=o_xn[d * 128:(d + 1) * 128, tsl],
                                in_=xn_f[:, d * TC:(d + 1) * TC]
                                ).then_inc(s_st.h, 16)
                s_st.n += DT * 16
                e.wait_ge(s_st.h, s_st.n)

            for qq in range(TC // 128 if upto >= 2 else 0):
                @body(blk, "tensor")
                def _(e):
                    p.wait(e, "tensor", s_ld, s_ld.n)
                    p.wait(e, "tensor", s_v, s_v.n)
                    for d in range(DT):
                        i = e.matmul(
                            psBc[:, 0:8],
                            xn_f[:, d * TC + qq * 128:d * TC + qq * 128 + 128],
                            wg_f[:, d * 8:(d + 1) * 8],
                            start=(d == 0), stop=(d == DT - 1))
                    inc(i, s_t)

                @body(blk, "scalar")
                def _(e):
                    p.wait(e, "scalar", s_t, s_t.n)
                    i = e.activation(gt_w[:], psBc[:, 0:8], AF.Sigmoid)
                    inc(i, s_a)

                @body(blk, "vector")
                def _(e):
                    p.wait(e, "vector", s_a, s_a.n)
                    m1, m2 = gt_t[:, 0:1], gt_t[:, 1:2]
                    den = gt_t[:, 2:3]
                    c1 = gt_t[:, 8:16]
                    cum = gt_t[:, 16:24]
                    kp = gt_t[:, 24:32]
                    g2 = gt_t[:, 32:40]
                    gl = gt_t[:, 40:48]
                    cum2 = gt_t[:, 48:56] if False else cum
                    V = lambda inst: sg(p, e, "vector", s_v, inst)
                    V(e.tensor_copy(gl, psBc[:, 0:8]))
                    V(e.reduce_max(m1, gl, axis=AX.X))
                    V(e.tensor_scalar(c1, gl, m1, None, ALU.is_equal))
                    V(e.tensor_tensor_scan(cum, ones8[:], c1, 0.0, ALU.mult,
                                           ALU.add))
                    V(e.tensor_single_scalar(kp, cum, 1.5, ALU.is_lt))
                    V(e.tensor_tensor(kp, kp, c1, ALU.mult))
                    V(e.scalar_tensor_tensor(g2, kp, NEG, gl, ALU.mult,
                                             ALU.add))
                    V(e.reduce_max(m2, g2, axis=AX.X))
                    V(e.tensor_scalar(c1, g2, m2, None, ALU.is_equal))
                    V(e.tensor_tensor_scan(cum, ones8[:], c1, 0.0, ALU.mult,
                                           ALU.add))
                    V(e.tensor_single_scalar(cum, cum, 1.5, ALU.is_lt))
                    V(e.tensor_tensor(c1, c1, cum, ALU.mult))
                    V(e.tensor_tensor(kp, kp, c1, ALU.add))
                    V(e.tensor_tensor(kp, kp, gt_w[:], ALU.mult))
                    V(e.reduce_sum(den, kp, axis=AX.X))
                    V(e.reciprocal(den, den))
                    i = e.tensor_scalar(kp, kp, den, None, ALU.mult)
                    inc(i, s_v)

                @body(blk, "tensor")
                def _(e):
                    p.wait(e, "tensor", s_v, s_v.n)
                    p.wait(e, "tensor", s_w, 4 * 16)
                    p.wait(e, "tensor", s_g, 1)
                    i = e.transpose(psBc2[0:8, 0:128], gt_t[:, 24:32],
                                    ident_f[:])
                    inc(i, s_t)

                @body(blk, "vector")
                def _(e):
                    p.wait(e, "vector", s_t, s_t.n)
                    i = e.tensor_copy(
                        wrow[0:8, tc * TC + qq * 128:tc * TC + qq * 128 + 128],
                        psBc2[0:8, 0:128])
                    inc(i, s_v)

            if upto < 3:
                continue
            @body(blk, "scalar")
            def _(e):
                p.wait(e, "scalar", s_v, s_v.n)
                i = e.activation(sq_bf[:], xn_f[:], AF.Square)
                inc(i, s_a)

            @body(blk, "tensor")
            def _(e):
                p.wait(e, "tensor", s_a, s_a.n)
                p.wait(e, "tensor", s_v, s_v.n)
                for d in range(DT):
                    i = e.matmul(psSt[0:1, :], ones_b[:],
                                 sq_bf[:, d * TC:(d + 1) * TC],
                                 start=(d == 0), stop=(d == DT - 1))
                inc(i, s_t)

            @body(blk, "scalar")
            def _(e):
                p.wait(e, "scalar", s_t, s_t.n)
                i = e.activation(rows[0:1, TC:2 * TC], psSt[0:1, :], AF.Sqrt)
                inc(i, s_a)

            @body(blk, "vector")
            def _(e):
                p.wait(e, "vector", s_a, s_a.n)
                r1 = rows[0:1, TC:2 * TC]
                r2 = rows[0:1, 2 * TC:3 * TC]
                sg(p, e, "vector", s_v,
                   e.tensor_single_scalar(r2, r1, 1e-12, ALU.max))
                sg(p, e, "vector", s_v, e.reciprocal(r1, r2))
                i = e.tensor_scalar(r1, r1, SQD, None, ALU.mult)
                inc(i, s_v)

            @body(blk, "tensor")
            def _(e):
                p.wait(e, "tensor", s_v, s_v.n)
                i = e.matmul(psBc[:, :], ones_f1[:], rows[0:1, TC:2 * TC],
                             start=True, stop=True)
                inc(i, s_t)

            @body(blk, "vector")
            def _(e):
                p.wait(e, "vector", s_t, s_t.n)
                p.wait(e, "vector", s_ld, 7 * 16)
                g1c = cols[:, 4 * DT:5 * DT]
                for d in range(DT):
                    dsl = slice(d * TC, (d + 1) * TC)
                    i = e.tensor_tensor(xn_f[:, dsl], xn_f[:, dsl],
                                        psBc[:, :], ALU.mult)
                    inc(i, s_v)
                for d in range(DT):
                    dsl = slice(d * TC, (d + 1) * TC)
                    p.wait(e, "vector", s_v, s_v.n)
                    i = e.tensor_scalar(x_bf[:, dsl], xn_f[:, dsl],
                                        g1c[:, d:d + 1], None, ALU.mult)
                    inc(i, s_v)

            if upto < 4:
                continue
            import os
            _evar = int(os.environ.get("L01A_EVAC", "0"))
            def evac_lin_s(e, o, ps):
                bl = cols[:, 0 * DT + o:0 * DT + o + 1]
                if _evar in (1,):
                    return e.activation(xgel_c[:, o * TC:(o + 1) * TC], ps[:, :],
                                        AF.Silu)
                return e.activation(xgel_c[:, o * TC:(o + 1) * TC], ps[:, :],
                                    AF.Gelu, bias=bl)

            def evac_lin_v(e, o, ps):
                bl = cols[:, 0 * DT + o:0 * DT + o + 1]
                if _evar in (1, 2):
                    return e.tensor_copy(lin_bf[:, o * TC:(o + 1) * TC], ps[:, :])
                return e.tensor_scalar(lin_bf[:, o * TC:(o + 1) * TC],
                                       ps[:, :], bl, None, ALU.add)

            mm(p, s_t, psA, DT, DT,
               lambda k, o: w4["W_lin"][:, k * D + o * 128:k * D + o * 128 + 128],
               lambda k: x_bf[:, k * TC:(k + 1) * TC],
               [("scalar", s_a, evac_lin_s), ("vector", s_v, evac_lin_v)],
               tdeps=[(s_v, s_v.n), (s_w, 4 * 16)],
               edeps=[(s_st, s_st.n), (s_ld, 7 * 16)])

            @body(blk, "sync")
            def _(e):
                p.wait(e, "sync", s_a, s_a.n)
                for d in range(DT):
                    e.dma_start(out=o_xgel[d * 128:(d + 1) * 128, tsl],
                                in_=xgel_c[:, d * TC:(d + 1) * TC]
                                ).then_inc(s_st.h, 16)
                s_st.n += DT * 16
                e.wait_ge(s_st.h, s_st.n)

            if upto < 5:
                continue
            def evac_rg(e, o, ps):
                bl = cols[:, 1 * DT + o:1 * DT + o + 1]
                e.tensor_scalar(rg_f[:, o * TC:(o + 1) * TC], ps[:, :],
                                bl, None, ALU.add)
                return e.tensor_scalar(rg_bf[:, o * TC:(o + 1) * TC], ps[:, :],
                                       bl, None, ALU.add)

            mm(p, s_t, psA, DT, DT,
               lambda k, o: w4["W_conv"][:, k * D + o * 128:k * D + o * 128 + 128],
               lambda k: lin_bf[:, k * TC:(k + 1) * TC],
               [("vector", s_v, evac_rg)],
               edeps=[(s_v, s_v.n), (s_ld, 7 * 16)])

            def evac_rt(e, o, ps):
                bl = cols[:, 2 * DT + o:2 * DT + o + 1]
                return e.activation(rt_f[:, o * TC:(o + 1) * TC], ps[:, :],
                                    AF.Sigmoid, bias=bl)

            mm(p, s_t, psA, DT, DT,
               lambda k, o: w4["Wa"][:, k * D + o * 128:k * D + o * 128 + 128],
               lambda k: rg_bf[:, k * TC:(k + 1) * TC],
               [("scalar", s_a, evac_rt)],
               edeps=[(s_v, s_v.n), (s_ld, 7 * 16)])

            @body(blk, "scalar")
            def _(e):
                p.wait(e, "scalar", s_v, s_v.n)
                for d in range(DT):
                    dsl = slice(d * TC, (d + 1) * TC)
                    i = e.activation(a_f[:, dsl], rt_f[:, dsl], AF.Exp,
                                     scale=lac[:, d:d + 1])
                    inc(i, s_a)

            def evac_it(e, o, ps):
                bl = cols[:, 3 * DT + o:3 * DT + o + 1]
                return e.activation(rt_f[:, o * TC:(o + 1) * TC], ps[:, :],
                                    AF.Sigmoid, bias=bl)

            mm(p, s_t, psA, DT, DT,
               lambda k, o: w4["Wx"][:, k * D + o * 128:k * D + o * 128 + 128],
               lambda k: rg_bf[:, k * TC:(k + 1) * TC],
               [("scalar", s_a, evac_it)], edeps=[(s_ld, 7 * 16)])

            @body(blk, "vector")
            def _(e):
                p.wait(e, "vector", s_a, s_a.n)
                i = e.tensor_tensor(gat_f[:], rt_f[:], rg_f[:], ALU.mult)
                inc(i, s_v)

            @body(blk, "vector")
            def _(e):
                p.wait(e, "vector", s_a, s_a.n)
                sg(p, e, "vector", s_v,
                   e.tensor_tensor(xn_f[:], a_f[:], a_f[:], ALU.mult))
                i = e.tensor_scalar(rg_f[:], xn_f[:], -1.0, 1.0,
                                    ALU.mult, ALU.add)
                inc(i, s_v)

            @body(blk, "scalar")
            def _(e):
                p.wait(e, "scalar", s_v, s_v.n)
                i = e.activation(rt_f[:], rg_f[:], AF.Sqrt)
                inc(i, s_a)

            @body(blk, "vector")
            def _(e):
                p.wait(e, "vector", s_a, s_a.n)
                i = e.tensor_tensor(gat_f[:], gat_f[:], rt_f[:], ALU.mult)
                inc(i, s_v)

            if _dbg and tc == NT - 1:
                @body(blk, "sync")
                def _(e):
                    p.wait(e, "sync", s_v, s_v.n)
                    p.wait(e, "sync", s_a, s_a.n)
                    e.dma_start(out=o_dba, in_=a_f[:]).then_inc(s_st.h, 16)
                    e.dma_start(out=o_dbg_g, in_=gat_f[:]).then_inc(s_st.h, 16)
                    e.dma_start(out=o_dbg_it, in_=rt_f[:]).then_inc(s_st.h, 16)
                    e.dma_start(out=o_dbg_rg, in_=rg_bf[:]).then_inc(s_st.h, 16)
                    e.dma_start(out=o_dbg_lin, in_=lin_bf[:]).then_inc(s_st.h, 16)
                    e.dma_start(out=o_dbg_sq, in_=rt_f[:]).then_inc(s_st.h, 16)
                    e.dma_start(out=o_dbg_si, in_=rg_f[:]).then_inc(s_st.h, 16)
                    e.dma_start(out=o_dbg_asq, in_=xn_f[:]).then_inc(s_st.h, 16)
                    e.dma_start(out=o_dbg_lac, in_=lac[:]).then_inc(s_st.h, 16)
                    e.dma_start(out=o_dbg_lam, in_=cols[:, 5 * DT:6 * DT]).then_inc(s_st.h, 16)
                    e.dma_start(out=o_dbg_cs, in_=cscr[:]).then_inc(s_st.h, 16)
                    e.dma_start(out=o_dbg_cs2, in_=cscr2[:]).then_inc(s_st.h, 16)
                    s_st.n += 192
                    e.wait_ge(s_st.h, s_st.n)

            if upto < 6:
                continue
            @body(blk, "vector")
            def _(e):
                p.wait(e, "vector", s_st, s_st.n)
                p.wait(e, "vector", s_v, s_v.n)
                for d in range(DT):
                    dsl = slice(d * TC, (d + 1) * TC)
                    ih = 0.0 if tc == 0 else carry[:, d:d + 1]
                    iP = 1.0 if tc == 0 else carry[:, DT + d:DT + d + 1]
                    i = e.tensor_tensor_scan(hs_f[:, dsl], a_f[:, dsl],
                                             gat_f[:, dsl], ih, ALU.mult,
                                             ALU.add)
                    inc(i, s_v)
                    i = e.tensor_tensor_scan(Ps_f[:, dsl], a_f[:, dsl],
                                             a_f[:, dsl], iP, ALU.mult,
                                             ALU.bypass)
                    inc(i, s_v)
                p.wait(e, "vector", s_v, s_v.n)
                for d in range(DT):
                    e.tensor_copy(carry[:, d:d + 1],
                                  hs_f[:, d * TC + TC - 1:d * TC + TC])
                    i = e.tensor_copy(carry[:, DT + d:DT + d + 1],
                                      Ps_f[:, d * TC + TC - 1:d * TC + TC])
                inc(i, s_v)

            @body(blk, "sync")
            def _(e):
                p.wait(e, "sync", s_v, s_v.n)
                for d in range(DT):
                    e.dma_start(out=o_hs[d * 128:(d + 1) * 128, tsl],
                                in_=hs_f[:, d * TC:(d + 1) * TC]
                                ).then_inc(s_st.h, 16)
                    e.dma_start(out=o_P[d * 128:(d + 1) * 128, tsl],
                                in_=Ps_f[:, d * TC:(d + 1) * TC]
                                ).then_inc(s_st.h, 16)
                s_st.n += 2 * DT * 16
                e.wait_ge(s_st.h, s_st.n)

        @body(blk, "sync")
        def _(e):
            p.wait(e, "sync", s_v, s_v.n)
            e.dma_start(out=o_wrow, in_=wrow[:, :]).then_inc(s_st.h, 16)
            s_st.n += 16
            e.wait_ge(s_st.h, s_st.n)
            e.wait_ge(s_st.h, s_st.n)

    return p


def build_L01b(upto=99):
    import os
    _step = int(os.environ.get("L01B_STEP", "99"))
    p = Pr("L01b")
    xn_d = p.dram_in("xn", [D, TL])
    wrow_d = p.dram_in("wrow", [8, TL])
    wsel_d = p.dram_in("wsel", [8, 8 * 128])
    Ws1_d = p.dram_in("Ws1", [D, 2048])
    Ws3_d = p.dram_in("Ws3", [D, 2048])
    Ws2_d = p.dram_in("Ws2", [2048, D])
    We1_d = p.dram_in("We1", [8, D, 2048])
    We3_d = p.dram_in("We3", [8, D, 2048])
    We2_d = p.dram_in("We2", [8, 2048, D])
    o_moe = p.dram_out("moe_y", [D, TL])

    with p.ctx:
        xn_bf = p.sbuf("xn_bf", [128, DT * TL], BF16)
        ones_f1 = p.sbuf("ones_f1", [1, 128])
        wsel_f = p.sbuf("wsel_f", [8, 8 * 128])
        wrow = p.sbuf("wrow_s", [8, TL])
        we1_b = p.sbuf("we1_b", [128, DT * 2048], BF16)
        we3_b = p.sbuf("we3_b", [128, DT * 2048], BF16)
        we2_b = p.sbuf("we2_b", [128, 16 * D], BF16)
        yacc = p.sbuf("yacc", [128, DT * TC])
        hbuf = p.sbuf("hbuf", [128, EDT * TC], BF16)
        moe_c = p.sbuf("moe_c", [128, DT * TC])

        psA = [p.psum(f"psA{i}", [128, TC]) for i in range(3)]
        psH1 = p.psum("psH1", [128, TC])
        psH2 = p.psum("psH2", [128, TC])
        psBc = p.psum("psBc", [128, TC])

        s_ld = p.sem("s_ld")
        s_w = p.sem("s_w")
        s_t = p.sem("s_t")
        s_v = p.sem("s_v")
        s_a = p.sem("s_a")
        s_st = p.sem("s_st")
        blk = p.open_block()

        @body(blk, "sync")
        def _(e):
            e.dma_start(out=wrow[:, :], in_=wrow_d).then_inc(s_ld.h, 16)
            e.dma_start(out=wsel_f[:, :], in_=wsel_d).then_inc(s_ld.h, 16)
            s_ld.n += 32
            e.wait_ge(s_ld.h, s_ld.n)

        @body(blk, "vector")
        def _(e):
            i = e.memset(ones_f1[:], 1.0)
            inc(i, s_v)

        @body(blk, "gpsimd")
        def _(e):
            e.dma_start(
                out=bass.AP(xn_bf, 0, [[DT * TL, 128], [TL, DT], [1, TL]]),
                in_=xn_d.rearrange("(k p) t -> p k t", p=128)
            ).then_inc(s_w.h, 16)
            s_w.n += 16

        xnw = s_w.n

        for tc in range(NT):
            tsl = slice(tc * TC, (tc + 1) * TC)

            @body(blk, "vector")
            def _(e):
                p.wait(e, "vector", s_st, s_st.n)
                i = e.memset(yacc[:], 0.0)
                inc(i, s_v)

            for ex in range(9 if upto >= 4 else (1 if upto >= 2 else 0)):
                we1s = We1_d[ex] if ex < 8 else Ws1_d
                we3s = We3_d[ex] if ex < 8 else Ws3_d
                we2s = We2_d[ex] if ex < 8 else Ws2_d
                z_prev = s_t.n

                @body(blk, "gpsimd")
                def _(e):
                    p.wait(e, "gpsimd", s_t, z_prev)
                    e.dma_start(
                        out=bass.AP(we1_b, 0,
                                    [[DT * 2048, 128], [2048, DT], [1, 2048]]),
                        in_=we1s.rearrange("(k p) f -> p k f", p=128)
                    ).then_inc(s_w.h, 16)
                    e.dma_start(
                        out=bass.AP(we3_b, 0,
                                    [[DT * 2048, 128], [2048, DT], [1, 2048]]),
                        in_=we3s.rearrange("(k p) f -> p k f", p=128)
                    ).then_inc(s_w.h, 16)
                    e.dma_start(
                        out=bass.AP(we2_b, 0,
                                    [[16 * D, 128], [D, 16], [1, D]]),
                        in_=we2s.rearrange("(k p) f -> p k f", p=128)
                    ).then_inc(s_w.h, 16)
                    s_w.n += 3 * 16

                wn = s_w.n

                if ex < 8 and _step >= 2:
                    @body(blk, "tensor")
                    def _(e):
                        p.wait(e, "tensor", s_ld, 32)
                        p.wait(e, "tensor", s_v, s_v.n)
                        i = e.matmul(psBc[:, :],
                                     wsel_f[:, ex * 128:(ex + 1) * 128],
                                     wrow[0:8, tsl], start=True, stop=True)
                        inc(i, s_t)
                wbct = s_t.n

                base_t = s_t.n
                base_a0 = s_a.n
                base_v0 = s_v.n
                if _step < 3:
                    continue

                nv = 2 if ex < 8 else 1

                @body(blk, "tensor")
                def _(e):
                    p.wait(e, "tensor", s_w, wn)
                    p.wait(e, "tensor", s_w, xnw)
                    p.wait(e, "tensor", s_a, base_a0)
                    p.wait(e, "tensor", s_v, base_v0)
                    for o in range(EDT):
                        if o >= 1:
                            p.wait(e, "tensor", s_v, base_v0 + nv * o)
                        for k in range(DT):
                            e.matmul(psH1[:, :],
                                     we1_b[:, k * 2048 + o * 128:
                                           k * 2048 + o * 128 + 128],
                                     xn_bf[:, k * TL + tc * TC:
                                           k * TL + (tc + 1) * TC],
                                     start=(k == 0), stop=(k == DT - 1))
                        for k in range(DT):
                            i = e.matmul(psH2[:, :],
                                         we3_b[:, k * 2048 + o * 128:
                                               k * 2048 + o * 128 + 128],
                                         xn_bf[:, k * TL + tc * TC:
                                               k * TL + (tc + 1) * TC],
                                         start=(k == 0), stop=(k == DT - 1))
                        inc(i, s_t)

                @body(blk, "scalar")
                def _(e):
                    p.wait(e, "scalar", s_t, z_prev)
                    for o in range(EDT):
                        p.wait(e, "scalar", s_t, base_t + o + 1)
                        i = e.activation(hbuf[:, o * TC:(o + 1) * TC],
                                         psH1[:, :], AF.Silu)
                        inc(i, s_a)

                @body(blk, "vector")
                def _(e):
                    for o in range(EDT):
                        p.wait(e, "vector", s_a, base_a0 + o + 1)
                        p.wait(e, "vector", s_t, base_t + o + 1)
                        hsl = slice(o * TC, (o + 1) * TC)
                        i = e.tensor_tensor(hbuf[:, hsl], hbuf[:, hsl],
                                            psH2[:, :], ALU.mult)
                        if ex < 8:
                            sg(p, e, "vector", s_v, i)
                            p.wait(e, "vector", s_t, wbct)
                            i = e.tensor_tensor(hbuf[:, hsl], hbuf[:, hsl],
                                                psBc[:, :], ALU.mult)
                        inc(i, s_v)

                hready = s_v.n
                zbase_t = s_t.n
                zbase_v = s_v.n
                if upto < 3:
                    continue

                zb = [psA[2], psH1, psH2]

                @body(blk, "tensor")
                def _(e):
                    p.wait(e, "tensor", s_v, hready)
                    p.wait(e, "tensor", s_a, base_a0 + EDT)
                    for o in range(DT):
                        if o >= 3:
                            p.wait(e, "tensor", s_v, zbase_v + o - 2)
                        ps = zb[o % 3]
                        for k in range(16):
                            i = e.matmul(ps[:, :],
                                         we2_b[:, k * D + o * 128:
                                               k * D + o * 128 + 128],
                                         hbuf[:, k * TC:(k + 1) * TC],
                                         start=(k == 0), stop=(k == 15))
                        inc(i, s_t)

                @body(blk, "vector")
                def _(e):
                    for o in range(DT):
                        p.wait(e, "vector", s_t, zbase_t + o + 1)
                        dsl = slice(o * TC, (o + 1) * TC)
                        i = e.tensor_tensor(yacc[:, dsl], yacc[:, dsl],
                                            zb[o % 3][:, :], ALU.add)
                        inc(i, s_v)

            @body(blk, "vector")
            def _(e):
                p.wait(e, "vector", s_v, s_v.n)
                i = e.tensor_scalar(moe_c[:], yacc[:], 1.0 / 3.0, None,
                                    ALU.mult)
                inc(i, s_v)

            @body(blk, "sync")
            def _(e):
                p.wait(e, "sync", s_v, s_v.n)
                for d in range(DT):
                    e.dma_start(out=o_moe[d * 128:(d + 1) * 128, tsl],
                                in_=moe_c[:, d * TC:(d + 1) * TC]
                                ).then_inc(s_st.h, 16)
                s_st.n += DT * 16
                e.wait_ge(s_st.h, s_st.n)

        @body(blk, "sync")
        def _(e):
            e.wait_ge(s_st.h, s_st.n)

    return p



def build_L02():
    p = Pr("L02")
    x0_d = p.dram_in("x0", [D, TL])
    xn_d = p.dram_in("xn", [D, TL])
    xgel_d = p.dram_in("xgel", [D, TL], BF16)
    hs_d = p.dram_in("hs_loc", [D, TL])
    P_d = p.dram_in("P_loc", [D, TL])
    moe_d = p.dram_in("moe_y", [D, TL])
    hend_d = p.dram_in("hend_g", [128, 3 * DT])
    pend_d = p.dram_in("pend_g", [128, 3 * DT])
    beta_d = p.dram_in("beta", [128, 3])
    gamma_d = p.dram_in("gamma", [128, 3])
    Wlin2_d = p.dram_in("L0_W_lin2", [D, D])
    W1_d = p.dram_in("L0_W1", [D, D])
    W2_d = p.dram_in("L0_W2", [D, D])
    W3_d = p.dram_in("L0_W3", [D, D])
    ip_d = p.dram_in("m0_inproj", [D, 2 * ED])
    cols_d = {nm: p.dram_in("c_" + nm, [128, DT])
              for nm in ["b_lin2", "b1", "b2", "b3", "g2", "normw0"]}

    o_x1 = p.dram_out("x1", [D, TL])
    o_xn1 = p.dram_out("xn1", [D, TL])
    o_xi = p.dram_out("xi0", [ED, TL])
    o_z = p.dram_out("z0", [ED, TL], BF16)

    W = DT * TC

    with p.ctx:
        F1 = p.sbuf("F1", [128, W])
        F2 = p.sbuf("F2", [128, W])
        F3 = p.sbuf("F3", [128, W])
        F4 = p.sbuf("F4", [128, W])
        F5 = p.sbuf("F5", [128, W])
        B1 = p.sbuf("B1", [128, W], BF16)
        B2 = p.sbuf("B2", [128, W], BF16)
        B3 = p.sbuf("B3", [128, W], BF16)
        XI = p.sbuf("XI", [128, 16 * TC])
        ZS = p.sbuf("ZS", [128, 16 * TC], BF16)
        wb = [p.sbuf(f"wb{i}", [128, DT * D], BF16) for i in range(2)]
        rows = p.sbuf("rows", [1, 4 * TC])
        ones_b = p.sbuf("ones_b", [128, 1], BF16)
        ones_f1 = p.sbuf("ones_f1", [1, 128])
        cols = p.sbuf("cols", [128, 6 * DT])
        hend = p.sbuf("hend", [128, 3 * DT])
        pend = p.sbuf("pend", [128, 3 * DT])
        bet = p.sbuf("bet", [128, 3])
        gam = p.sbuf("gam", [128, 3])
        hin = p.sbuf("hin", [128, DT])
        csc = p.sbuf("csc", [128, DT])

        psA = [p.psum(f"psA{i}", [128, TC]) for i in range(3)]
        psSt = p.psum("psSt", [1, TC])
        psBc = p.psum("psBc", [128, TC])
        psBc2 = p.psum("psBc2", [128, TC])

        s_ld = p.sem("s_ld")
        s_w = p.sem("s_w")
        s_t = p.sem("s_t")
        s_v = p.sem("s_v")
        s_a = p.sem("s_a")
        s_st = p.sem("s_st")
        blk = p.open_block()

        @body(blk, "sync")
        def _(e):
            for j, nm in enumerate(["b_lin2", "b1", "b2", "b3", "g2",
                                    "normw0"]):
                e.dma_start(out=cols[:, j * DT:(j + 1) * DT],
                            in_=cols_d[nm]).then_inc(s_ld.h, 16)
            e.dma_start(out=hend[:], in_=hend_d).then_inc(s_ld.h, 16)
            e.dma_start(out=pend[:], in_=pend_d).then_inc(s_ld.h, 16)
            e.dma_start(out=bet[:], in_=beta_d).then_inc(s_ld.h, 16)
            e.dma_start(out=gam[:], in_=gamma_d).then_inc(s_ld.h, 16)
            s_ld.n += 10 * 16
            e.wait_ge(s_ld.h, s_ld.n)

        @body(blk, "vector")
        def _(e):
            e.memset(ones_b[:], 1.0)
            i = e.memset(ones_f1[:], 1.0)
            inc(i, s_v)
            p.wait(e, "vector", s_ld, s_ld.n)
            sg(p, e, "vector", s_v, e.memset(hin[:], 0.0))
            for j in range(3):
                gj = gam[:, j:j + 1]
                bj = bet[:, j:j + 1]
                pj = pend[:, j * DT:(j + 1) * DT]
                hj = hend[:, j * DT:(j + 1) * DT]
                sg(p, e, "vector", s_v,
                   e.tensor_scalar(csc[:], pj, gj, None, ALU.mult))
                sg(p, e, "vector", s_v,
                   e.tensor_scalar(F1[:, 0:1], gam[:, j:j + 1], -1.0, 1.0,
                                   ALU.mult, ALU.add))
                sg(p, e, "vector", s_v,
                   e.tensor_scalar(csc[:], csc[:], F1[:, 0:1], None,
                                   ALU.add))
                sg(p, e, "vector", s_v,
                   e.tensor_tensor(hin[:], hin[:], csc[:], ALU.mult))
                sg(p, e, "vector", s_v,
                   e.tensor_scalar(csc[:], hj, bj, None, ALU.mult))
                sg(p, e, "vector", s_v,
                   e.tensor_tensor(hin[:], hin[:], csc[:], ALU.add))

        wload_t = [0, 0]

        def wstream(idx, dram, coloff, width):
            @body(blk, "gpsimd")
            def _(e):
                p.wait(e, "gpsimd", s_t, wload_t[idx])
                e.dma_start(
                    out=bass.AP(wb[idx], 0,
                                [[DT * D, 128], [width, DT], [1, width]]),
                    in_=dram.rearrange("(k p) o -> p k o", p=128)[
                        :, :, coloff:coloff + width]
                ).then_inc(s_w.h, 16)
                s_w.n += 16
                e.wait_ge(s_w.h, s_w.n)
            return s_w.n

        for tc in range(NT):
            tsl = slice(tc * TC, (tc + 1) * TC)

            @body(blk, "sync")
            def _(e):
                p.wait(e, "sync", s_v, s_v.n)
                p.wait(e, "sync", s_st, s_st.n)
                for d in range(DT):
                    e.dma_start(out=F1[:, d * TC:(d + 1) * TC],
                                in_=hs_d[d * 128:(d + 1) * 128, tsl]
                                ).then_inc(s_ld.h, 16)
                    e.dma_start(out=F2[:, d * TC:(d + 1) * TC],
                                in_=P_d[d * 128:(d + 1) * 128, tsl]
                                ).then_inc(s_ld.h, 16)
                    e.dma_start(out=B1[:, d * TC:(d + 1) * TC],
                                in_=xgel_d[d * 128:(d + 1) * 128, tsl]
                                ).then_inc(s_ld.h, 16)
                    e.dma_start(out=F3[:, d * TC:(d + 1) * TC],
                                in_=xn_d[d * 128:(d + 1) * 128, tsl]
                                ).then_inc(s_ld.h, 16)
                s_ld.n += 4 * DT * 16
                e.wait_ge(s_ld.h, s_ld.n)

            w_lin2 = wstream(0, Wlin2_d, 0, D)

            @body(blk, "vector")
            def _(e):
                p.wait(e, "vector", s_ld, s_ld.n)
                for d in range(DT):
                    dsl = slice(d * TC, (d + 1) * TC)
                    i = e.scalar_tensor_tensor(F1[:, dsl], F2[:, dsl],
                                               hin[:, d:d + 1], F1[:, dsl],
                                               ALU.mult, ALU.add)
                    inc(i, s_v)
                for d in range(DT):
                    dsl = slice(d * TC, (d + 1) * TC)
                    p.wait(e, "vector", s_v, s_v.n)
                    i = e.tensor_tensor(B2[:, dsl], B1[:, dsl], F1[:, dsl],
                                        ALU.mult)
                    inc(i, s_v)

            def evac_xr(e, o, ps):
                sg(p, e, "vector", s_v,
                   e.tensor_scalar(F4[:, o * TC:(o + 1) * TC], ps[:, :],
                                   cols[:, 0 * DT + o:0 * DT + o + 1], None,
                                   ALU.add))
                return e.tensor_tensor(F1[:, o * TC:(o + 1) * TC],
                                       F4[:, o * TC:(o + 1) * TC],
                                       F3[:, o * TC:(o + 1) * TC], ALU.add)

            mm(p, s_t, psA, DT, DT,
               lambda k, o: wb[0][:, k * D + o * 128:k * D + o * 128 + 128],
               lambda k: B2[:, k * TC:(k + 1) * TC],
               [("vector", s_v, evac_xr, 2)],
               tdeps=[(s_v, s_v.n), (s_w, w_lin2)])
            wload_t[0] = s_t.n

            w_w1 = wstream(1, W1_d, 0, D)

            @body(blk, "vector")
            def _(e):
                p.wait(e, "vector", s_v, s_v.n)
                for d in range(DT):
                    dsl = slice(d * TC, (d + 1) * TC)
                    i = e.tensor_tensor(B1[:, dsl], F1[:, dsl], F1[:, dsl],
                                        ALU.mult)
                    inc(i, s_v)

            @body(blk, "tensor")
            def _(e):
                p.wait(e, "tensor", s_v, s_v.n)
                for d in range(DT):
                    i = e.matmul(psSt[0:1, :], ones_b[:],
                                 B1[:, d * TC:(d + 1) * TC],
                                 start=(d == 0), stop=(d == DT - 1))
                inc(i, s_t)

            @body(blk, "scalar")
            def _(e):
                p.wait(e, "scalar", s_t, s_t.n)
                i = e.activation(rows[0:1, TC:2 * TC], psSt[0:1, :], AF.Sqrt)
                inc(i, s_a)

            @body(blk, "vector")
            def _(e):
                p.wait(e, "vector", s_a, s_a.n)
                r1 = rows[0:1, TC:2 * TC]
                r2 = rows[0:1, 2 * TC:3 * TC]
                sg(p, e, "vector", s_v,
                   e.tensor_single_scalar(r2, r1, 1e-12, ALU.max))
                sg(p, e, "vector", s_v, e.reciprocal(r1, r2))
                i = e.tensor_scalar(r1, r1, SQD, None, ALU.mult)
                inc(i, s_v)

            @body(blk, "tensor")
            def _(e):
                p.wait(e, "tensor", s_v, s_v.n)
                i = e.matmul(psBc[:, :], ones_f1[:], rows[0:1, TC:2 * TC],
                             start=True, stop=True)
                inc(i, s_t)

            @body(blk, "vector")
            def _(e):
                p.wait(e, "vector", s_t, s_t.n)
                g2c = cols[:, 4 * DT:5 * DT]
                for d in range(DT):
                    dsl = slice(d * TC, (d + 1) * TC)
                    i = e.tensor_tensor(F2[:, dsl], F1[:, dsl], psBc[:, :],
                                        ALU.mult)
                    inc(i, s_v)
                for d in range(DT):
                    dsl = slice(d * TC, (d + 1) * TC)
                    p.wait(e, "vector", s_v, s_v.n)
                    i = e.tensor_scalar(B3[:, dsl], F2[:, dsl],
                                        g2c[:, d:d + 1], None, ALU.mult)
                    inc(i, s_v)

            def evac_sig(e, o, ps):
                return e.activation(F2[:, o * TC:(o + 1) * TC], ps[:, :],
                                    AF.Sigmoid,
                                    bias=cols[:, 1 * DT + o:1 * DT + o + 1])

            mm(p, s_t, psA, DT, DT,
               lambda k, o: wb[1][:, k * D + o * 128:k * D + o * 128 + 128],
               lambda k: B3[:, k * TC:(k + 1) * TC],
               [("scalar", s_a, evac_sig)],
               tdeps=[(s_v, s_v.n), (s_w, w_w1)])
            wload_t[1] = s_t.n

            w_w2 = wstream(0, W2_d, 0, D)

            @body(blk, "scalar")
            def _(e):
                p.wait(e, "scalar", s_a, s_a.n)
                for d in range(DT):
                    dsl = slice(d * TC, (d + 1) * TC)
                    i = e.activation(B1[:, dsl], F2[:, dsl], AF.Gelu)
                    inc(i, s_a)

            def evac_comb(e, o, ps):
                return e.scalar_tensor_tensor(
                    B2[:, o * TC:(o + 1) * TC], ps[:, :],
                    cols[:, 2 * DT + o:2 * DT + o + 1],
                    B1[:, o * TC:(o + 1) * TC], ALU.add, ALU.mult)

            mm(p, s_t, psA, DT, DT,
               lambda k, o: wb[0][:, k * D + o * 128:k * D + o * 128 + 128],
               lambda k: B3[:, k * TC:(k + 1) * TC],
               [("vector", s_v, evac_comb)],
               tdeps=[(s_a, s_a.n), (s_w, w_w2)])
            wload_t[0] = s_t.n

            w_w3 = wstream(1, W3_d, 0, D)

            def evac_t1(e, o, ps):
                return e.scalar_tensor_tensor(
                    F2[:, o * TC:(o + 1) * TC], ps[:, :],
                    cols[:, 3 * DT + o:3 * DT + o + 1],
                    F1[:, o * TC:(o + 1) * TC], ALU.add, ALU.add)

            mm(p, s_t, psA, DT, DT,
               lambda k, o: wb[1][:, k * D + o * 128:k * D + o * 128 + 128],
               lambda k: B2[:, k * TC:(k + 1) * TC],
               [("vector", s_v, evac_t1)],
               tdeps=[(s_v, s_v.n), (s_w, w_w3)])
            wload_t[1] = s_t.n

            @body(blk, "sync")
            def _(e):
                p.wait(e, "sync", s_v, s_v.n)
                for d in range(DT):
                    e.dma_start(out=F3[:, d * TC:(d + 1) * TC],
                                in_=moe_d[d * 128:(d + 1) * 128, tsl]
                                ).then_inc(s_ld.h, 16)
                    e.dma_start(out=F4[:, d * TC:(d + 1) * TC],
                                in_=x0_d[d * 128:(d + 1) * 128, tsl]
                                ).then_inc(s_ld.h, 16)
                s_ld.n += 2 * DT * 16
                e.wait_ge(s_ld.h, s_ld.n)

            @body(blk, "vector")
            def _(e):
                p.wait(e, "vector", s_ld, s_ld.n)
                p.wait(e, "vector", s_v, s_v.n)
                for d in range(DT):
                    dsl = slice(d * TC, (d + 1) * TC)
                    i = e.tensor_tensor(F5[:, dsl], F2[:, dsl], F3[:, dsl],
                                        ALU.add)
                    inc(i, s_v)
                for d in range(DT):
                    dsl = slice(d * TC, (d + 1) * TC)
                    p.wait(e, "vector", s_v, s_v.n)
                    i = e.tensor_tensor(F5[:, dsl], F5[:, dsl], F4[:, dsl],
                                        ALU.add)
                    inc(i, s_v)

            x1v = s_v.n

            @body(blk, "sync")
            def _(e):
                p.wait(e, "sync", s_v, x1v)
                for d in range(DT):
                    e.dma_start(out=o_x1[d * 128:(d + 1) * 128, tsl],
                                in_=F5[:, d * TC:(d + 1) * TC]
                                ).then_inc(s_st.h, 16)
                s_st.n += DT * 16
                e.wait_ge(s_st.h, s_st.n)

            @body(blk, "scalar")
            def _(e):
                p.wait(e, "scalar", s_v, x1v)
                i = e.activation(B1[:], F5[:], AF.Copy)
                inc(i, s_a)
                i = e.activation(B2[:], F5[:], AF.Square)
                inc(i, s_a)

            @body(blk, "tensor")
            def _(e):
                p.wait(e, "tensor", s_a, s_a.n)
                p.wait(e, "tensor", s_v, s_v.n)
                for d in range(DT):
                    i = e.matmul(psSt[0:1, :], ones_b[:],
                                 B1[:, d * TC:(d + 1) * TC],
                                 start=(d == 0), stop=(d == DT - 1))
                inc(i, s_t)

            @body(blk, "vector")
            def _(e):
                p.wait(e, "vector", s_t, s_t.n)
                i = e.tensor_scalar(rows[0:1, 0:TC], psSt[0:1, :], 1.0 / D,
                                    None, ALU.mult)
                inc(i, s_v)

            @body(blk, "tensor")
            def _(e):
                p.wait(e, "tensor", s_v, s_v.n)
                for d in range(DT):
                    i = e.matmul(psSt[0:1, :], ones_b[:],
                                 B2[:, d * TC:(d + 1) * TC],
                                 start=(d == 0), stop=(d == DT - 1))
                inc(i, s_t)

            @body(blk, "vector")
            def _(e):
                p.wait(e, "vector", s_t, s_t.n)
                r0 = rows[0:1, 0:TC]
                r1 = rows[0:1, TC:2 * TC]
                r2 = rows[0:1, 2 * TC:3 * TC]
                e.tensor_scalar(r1, psSt[0:1, :], 1.0 / D, None, ALU.mult)
                sg(p, e, "vector", s_v, e.tensor_tensor(r2, r0, r0, ALU.mult))
                sg(p, e, "vector", s_v,
                   e.tensor_tensor(r2, r1, r2, ALU.subtract))
                i = e.tensor_scalar(r1, r2, 1e-5, None, ALU.add)
                inc(i, s_v)

            @body(blk, "scalar")
            def _(e):
                p.wait(e, "scalar", s_v, s_v.n)
                r1 = rows[0:1, TC:2 * TC]
                r2 = rows[0:1, 2 * TC:3 * TC]
                sg(p, e, "scalar", s_a, e.activation(r2, r1, AF.Ln))
                i = e.activation(r1, r2, AF.Exp, scale=-0.5)
                inc(i, s_a)

            @body(blk, "vector")
            def _(e):
                p.wait(e, "vector", s_a, s_a.n)
                r0 = rows[0:1, 0:TC]
                r1 = rows[0:1, TC:2 * TC]
                r2 = rows[0:1, 2 * TC:3 * TC]
                r3 = rows[0:1, 3 * TC:4 * TC]
                sg(p, e, "vector", s_v, e.tensor_tensor(r2, r0, r1, ALU.mult))
                i = e.tensor_scalar(r3, r2, -1.0, None, ALU.mult)
                inc(i, s_v)

            @body(blk, "tensor")
            def _(e):
                p.wait(e, "tensor", s_v, s_v.n)
                e.matmul(psBc[:, :], ones_f1[:], rows[0:1, TC:2 * TC],
                         start=True, stop=True)
                i = e.matmul(psBc2[:, :], ones_f1[:], rows[0:1, 3 * TC:4 * TC],
                             start=True, stop=True)
                inc(i, s_t)

            @body(blk, "vector")
            def _(e):
                p.wait(e, "vector", s_t, s_t.n)
                for d in range(DT):
                    dsl = slice(d * TC, (d + 1) * TC)
                    i = e.tensor_tensor(F3[:, dsl], F5[:, dsl], psBc[:, :],
                                        ALU.mult)
                    inc(i, s_v)
                for d in range(DT):
                    dsl = slice(d * TC, (d + 1) * TC)
                    p.wait(e, "vector", s_v, s_v.n)
                    i = e.tensor_tensor(F3[:, dsl], F3[:, dsl], psBc2[:, :],
                                        ALU.add)
                    inc(i, s_v)

            xn1v = s_v.n

            @body(blk, "sync")
            def _(e):
                p.wait(e, "sync", s_v, xn1v)
                for d in range(DT):
                    e.dma_start(out=o_xn1[d * 128:(d + 1) * 128, tsl],
                                in_=F3[:, d * TC:(d + 1) * TC]
                                ).then_inc(s_st.h, 16)
                s_st.n += DT * 16
                e.wait_ge(s_st.h, s_st.n)

            @body(blk, "vector")
            def _(e):
                p.wait(e, "vector", s_v, s_v.n)
                for d in range(DT):
                    dsl = slice(d * TC, (d + 1) * TC)
                    i = e.tensor_tensor(B2[:, dsl], F3[:, dsl], F3[:, dsl],
                                        ALU.mult)
                    inc(i, s_v)

            @body(blk, "tensor")
            def _(e):
                p.wait(e, "tensor", s_v, s_v.n)
                for d in range(DT):
                    i = e.matmul(psSt[0:1, :], ones_b[:],
                                 B2[:, d * TC:(d + 1) * TC],
                                 start=(d == 0), stop=(d == DT - 1))
                inc(i, s_t)

            @body(blk, "vector")
            def _(e):
                p.wait(e, "vector", s_t, s_t.n)
                r1 = rows[0:1, TC:2 * TC]
                i = e.tensor_scalar(r1, psSt[0:1, :], 1.0 / D, None, ALU.mult)
                sg(p, e, "vector", s_v, i)
                i = e.tensor_scalar(r1, r1, 1e-5, None, ALU.add)
                inc(i, s_v)

            @body(blk, "scalar")
            def _(e):
                p.wait(e, "scalar", s_v, s_v.n)
                r1 = rows[0:1, TC:2 * TC]
                r2 = rows[0:1, 2 * TC:3 * TC]
                sg(p, e, "scalar", s_a, e.activation(r2, r1, AF.Ln))
                i = e.activation(r1, r2, AF.Exp, scale=-0.5)
                inc(i, s_a)

            @body(blk, "tensor")
            def _(e):
                p.wait(e, "tensor", s_a, s_a.n)
                i = e.matmul(psBc[:, :], ones_f1[:], rows[0:1, TC:2 * TC],
                             start=True, stop=True)
                inc(i, s_t)

            @body(blk, "vector")
            def _(e):
                p.wait(e, "vector", s_t, s_t.n)
                nwc = cols[:, 5 * DT:6 * DT]
                for d in range(DT):
                    dsl = slice(d * TC, (d + 1) * TC)
                    i = e.tensor_tensor(F4[:, dsl], F3[:, dsl], psBc[:, :],
                                        ALU.mult)
                    inc(i, s_v)
                for d in range(DT):
                    dsl = slice(d * TC, (d + 1) * TC)
                    p.wait(e, "vector", s_v, s_v.n)
                    i = e.tensor_scalar(B3[:, dsl], F4[:, dsl],
                                        nwc[:, d:d + 1], None, ALU.mult)
                    inc(i, s_v)

            for qq in range(4):
                widx = qq % 2
                wq = wstream(widx, ip_d, qq * 1024, 1024)

                def evac_ip(e, o, ps, qq=qq):
                    go = qq * 8 + o
                    if go < 16:
                        return e.tensor_copy(XI[:, go * TC:(go + 1) * TC],
                                             ps[:, :])
                    return e.tensor_copy(ZS[:, (go - 16) * TC:
                                            (go - 15) * TC], ps[:, :])

                mm(p, s_t, psA, DT, DT,
                   lambda k, o, widx=widx: wb[widx][:, k * D + o * 128:
                                                    k * D + o * 128 + 128],
                   lambda k: B3[:, k * TC:(k + 1) * TC],
                   [("vector", s_v, evac_ip)],
                   tdeps=[(s_v, s_v.n), (s_w, wq)])
                wload_t[widx] = s_t.n

            @body(blk, "sync")
            def _(e):
                p.wait(e, "sync", s_v, s_v.n)
                for g in range(16):
                    e.dma_start(out=o_xi[g * 128:(g + 1) * 128, tsl],
                                in_=XI[:, g * TC:(g + 1) * TC]
                                ).then_inc(s_st.h, 16)
                    e.dma_start(out=o_z[g * 128:(g + 1) * 128, tsl],
                                in_=ZS[:, g * TC:(g + 1) * TC]
                                ).then_inc(s_st.h, 16)
                s_st.n += 32 * 16
                e.wait_ge(s_st.h, s_st.n)

        @body(blk, "sync")
        def _(e):
            e.wait_ge(s_st.h, s_st.n)

    return p




def build_MA(l):
    p = Pr(f"MA{l}")
    p._sc_ack = 0
    xi_d = p.dram_in("xi", [ED, TL])
    tail_d = p.dram_in("xi_tail", [ED, 4])
    xproj_d = p.dram_in(f"m{l}_xproj", [ED, 96])
    dtw_d = p.dram_in(f"m{l}_dtw", [64, ED])
    alog_d = p.dram_in(f"m{l}_Alog", [128, EDT * NST])
    sel32_d = p.dram_in("sel32", [32, 32 * 128], BF16)
    cw_d = {k: p.dram_in(f"m{l}_cw{k}", [128, EDT]) for k in range(KCONV)}
    cb_d = p.dram_in(f"m{l}_convb", [128, EDT])
    dtb_d = p.dram_in(f"m{l}_dtb", [128, EDT])

    o_u = p.dram_out("u_bf", [ED, TL], BF16)
    o_dl = p.dram_out("delta", [ED, TL])
    o_yl = p.dram_out("ylocal", [ED, TL])
    o_c = p.dram_out("c_rows", [16, TL])
    o_he = p.dram_out("hend_m", [128, EDT * NST])
    o_sd = p.dram_out("sumd", [128, EDT])

    W16 = EDT * TC

    with p.ctx:
        XIC = p.sbuf("XIC", [128, EDT * 528])
        UC = p.sbuf("UC", [128, W16])
        UB = p.sbuf("UB", [128, W16], BF16)
        DC = p.sbuf("DC", [128, W16])
        YC = p.sbuf("YC", [128, W16])
        HB = XIC
        BBC = p.sbuf("BBC", [128, 2 * NST * TC], BF16)
        DBC = p.sbuf("DBC", [128, TC])
        DBCB = p.sbuf("DBCB", [128, TC], BF16)
        BCR = p.sbuf("BCR", [32, TC], BF16)
        S1 = p.sbuf("S1", [128, TC])
        S2 = p.sbuf("S2", [128, TC])
        S3 = p.sbuf("S3", [128, TC])
        carry = p.sbuf("carry", [128, EDT * NST])
        sumd = p.sbuf("sumd_s", [128, EDT])
        cols = p.sbuf("colsm", [128, (KCONV + 2) * EDT])
        alog = p.sbuf("alog", [128, EDT * NST])
        acol = p.sbuf("acol", [128, EDT * NST])
        sel32 = p.sbuf("sel32_s", [32, 32 * 128], BF16)
        wbx = p.sbuf("wbx", [128, EDT * 96], BF16)
        wbd = p.sbuf("wbd", [64, ED], BF16)
        rone = p.sbuf("rone", [128, 1])

        psA = [p.psum(f"psA{i}", [128, TC]) for i in range(3)]
        psB = [p.psum(f"psB{i}", [128, TC]) for i in range(3)]

        s_ld = p.sem("s_ld")
        s_w = p.sem("s_w")
        s_g = p.sem("s_g")
        s_t = p.sem("s_t")
        s_v = p.sem("s_v")
        s_a = p.sem("s_a")
        s_st = p.sem("s_st")
        blk = p.open_block()

        @body(blk, "sync")
        def _(e):
            for k in range(KCONV):
                e.dma_start(out=cols[:, k * EDT:(k + 1) * EDT],
                            in_=cw_d[k]).then_inc(s_ld.h, 16)
            e.dma_start(out=cols[:, KCONV * EDT:(KCONV + 1) * EDT],
                        in_=cb_d).then_inc(s_ld.h, 16)
            e.dma_start(out=cols[:, (KCONV + 1) * EDT:(KCONV + 2) * EDT],
                        in_=dtb_d).then_inc(s_ld.h, 16)
            e.dma_start(out=alog[:], in_=alog_d).then_inc(s_ld.h, 16)
            s_ld.n += (KCONV + 3) * 16
            e.wait_ge(s_ld.h, s_ld.n)

        @body(blk, "gpsimd")
        def _(e):
            e.dma_start(out=sel32[:], in_=sel32_d).then_inc(s_w.h, 16)
            s_w.n += 16
            e.wait_ge(s_w.h, s_w.n)
            e.dma_start(
                out=bass.AP(wbx, 0, [[EDT * 96, 128], [96, EDT], [1, 96]]),
                in_=xproj_d.rearrange("(k p) o -> p k o", p=128)
            ).then_inc(s_w.h, 16)
            s_w.n += 16
            e.wait_ge(s_w.h, s_w.n)
            e.dma_start(out=wbd[:, :], in_=dtw_d).then_inc(s_w.h, 16)
            s_w.n += 16
            e.wait_ge(s_w.h, s_w.n)

        @body(blk, "scalar")
        def _(e):
            p.wait(e, "scalar", s_ld, s_ld.n)
            i = e.activation(acol[:], alog[:], AF.Exp)
            inc(i, s_a)

        @body(blk, "vector")
        def _(e):
            p.wait(e, "vector", s_a, s_a.n)
            sg(p, e, "vector", s_v,
               e.tensor_scalar(acol[:], acol[:], -1.0, None, ALU.mult))
            e.memset(carry[:], 0.0)
            e.memset(sumd[:], 0.0)
            i = e.memset(rone[:], 1.0)
            inc(i, s_v)

        for tc in range(NT):
            tsl = slice(tc * TC, (tc + 1) * TC)

            @body(blk, "sync")
            def _(e):
                p.wait(e, "sync", s_v, s_v.n)
                p.wait(e, "sync", s_a, s_a.n)
                p.wait(e, "sync", s_st, s_st.n)
                for g in range(EDT):
                    if tc == 0:
                        e.dma_start(out=XIC[:, g * 528:g * 528 + 3],
                                    in_=tail_d[g * 128:(g + 1) * 128, 1:4]
                                    ).then_inc(s_ld.h, 16)
                    else:
                        e.dma_start(out=XIC[:, g * 528:g * 528 + 3],
                                    in_=xi_d[g * 128:(g + 1) * 128,
                                             tc * TC - 3:tc * TC]
                                    ).then_inc(s_ld.h, 16)
                    e.dma_start(out=XIC[:, g * 528 + 3:g * 528 + 3 + TC],
                                in_=xi_d[g * 128:(g + 1) * 128, tsl]
                                ).then_inc(s_ld.h, 16)
                s_ld.n += 2 * EDT * 16
                e.wait_ge(s_ld.h, s_ld.n)

            @body(blk, "vector")
            def _(e):
                p.wait(e, "vector", s_ld, s_ld.n)
                p.wait(e, "vector", s_a, s_a.n)
                for g in range(EDT):
                    xb = g * 528
                    usl = slice(g * TC, (g + 1) * TC)
                    i = e.tensor_scalar(UC[:, usl], XIC[:, xb:xb + TC],
                                        cols[:, 0 * EDT + g:0 * EDT + g + 1],
                                        None, ALU.mult)
                    inc(i, s_v)
                    for k in range(1, KCONV):
                        p.wait(e, "vector", s_v, s_v.n)
                        i = e.scalar_tensor_tensor(
                            UC[:, usl], XIC[:, xb + k:xb + k + TC],
                            cols[:, k * EDT + g:k * EDT + g + 1],
                            UC[:, usl], ALU.mult, ALU.add)
                        inc(i, s_v)

            @body(blk, "scalar")
            def _(e):
                p.wait(e, "scalar", s_v, s_v.n)
                cb = cols[:, KCONV * EDT:(KCONV + 1) * EDT]
                for g in range(EDT):
                    usl = slice(g * TC, (g + 1) * TC)
                    p.wait(e, "scalar", s_a, s_a.n)
                    i = e.activation(S1[:, 0:TC], UC[:, usl], AF.Silu,
                                     bias=cb[:, g:g + 1])
                    inc(i, s_a)
                    p.wait(e, "scalar", s_a, s_a.n)
                    i = e.activation(UB[:, usl], S1[:, 0:TC], AF.Copy)
                    inc(i, s_a)
                    i = e.activation(UC[:, usl], S1[:, 0:TC], AF.Copy)
                    inc(i, s_a)

            ucv = s_a.n

            @body(blk, "sync")
            def _(e):
                p.wait(e, "sync", s_a, ucv)
                for g in range(EDT):
                    e.dma_start(out=o_u[g * 128:(g + 1) * 128, tsl],
                                in_=UB[:, g * TC:(g + 1) * TC]
                                ).then_inc(s_st.h, 16)
                s_st.n += EDT * 16
                e.wait_ge(s_st.h, s_st.n)

            @body(blk, "tensor")
            def _(e):
                p.wait(e, "tensor", s_a, ucv)
                p.wait(e, "tensor", s_w, s_w.n)
                p.wait(e, "tensor", s_v, s_v.n)
                for k in range(EDT):
                    i = e.matmul(psA[0][0:96, :],
                                 wbx[:, k * 96:(k + 1) * 96],
                                 UB[:, k * TC:(k + 1) * TC],
                                 start=(k == 0), stop=(k == EDT - 1))
                inc(i, s_t)

            @body(blk, "vector")
            def _(e):
                p.wait(e, "vector", s_t, s_t.n)
                sg(p, e, "vector", s_v,
                   e.tensor_copy(DBC[0:96, :], psA[0][0:96, :]))
                sg(p, e, "vector", s_v,
                   e.tensor_copy(DBCB[0:96, :], DBC[0:96, :]))
                i = e.tensor_copy(BCR[0:32, :], DBCB[64:96, :])
                inc(i, s_v)

            dbcv = s_v.n

            def evac_dl(e, o, ps):
                dtb = cols[:, (KCONV + 1) * EDT + o:(KCONV + 1) * EDT + o + 1]
                sg(p, e, "vector", s_v,
                   e.tensor_scalar(S1[:, 0:TC], ps[:, :], dtb, None, ALU.add))
                return e.tensor_scalar_max(S2[:, 0:TC], S1[:, 0:TC], 0.0)

            dl_a0 = s_a.n

            @body(blk, "tensor")
            def _(e):
                p.wait(e, "tensor", s_v, dbcv)
                for o in range(EDT):
                    if o >= 3:
                        p.wait(e, "tensor", s_a, dl_a0 + 4 * (o - 2))
                    i = e.matmul(psB[o % 3][:, :],
                                 wbd[0:64, o * 128:(o + 1) * 128],
                                 DBCB[0:64, :],
                                 start=True, stop=True)
                    inc(i, s_t)
            dl_t0 = s_t.n - EDT

            p._dl_acks = []
            dl_v0 = s_v.n

            @body(blk, "scalar")
            def _(e):
                for o in range(EDT):
                    p.wait(e, "scalar", s_t, dl_t0 + o + 1)
                    if o >= 1:
                        p.wait(e, "scalar", s_v, dl_v0 + 3 * o - 2)
                    dtb = cols[:, (KCONV + 1) * EDT + o:
                               (KCONV + 1) * EDT + o + 1]
                    i = e.activation(S1[:, 0:TC], psB[o % 3][:, :], AF.Abs,
                                     bias=dtb)
                    inc(i, s_a)
                    p.wait(e, "scalar", s_a, s_a.n)
                    i = e.activation(S2[:, 0:TC], S1[:, 0:TC], AF.Exp,
                                     scale=-1.0)
                    inc(i, s_a)
                    p.wait(e, "scalar", s_a, s_a.n)
                    i = e.activation(S1[:, 0:TC], S2[:, 0:TC], AF.Ln,
                                     bias=1.0)
                    inc(i, s_a)
                    p.wait(e, "scalar", s_a, s_a.n)
                    i = e.activation(S2[:, 0:TC], psB[o % 3][:, :], AF.Relu,
                                     bias=dtb)
                    inc(i, s_a)
                    p._dl_acks.append(s_a.n)

            @body(blk, "vector")
            def _(e):
                for o in range(EDT):
                    p.wait(e, "vector", s_a, p._dl_acks[o])
                    i = e.tensor_tensor(DC[:, o * TC:(o + 1) * TC],
                                        S1[:, 0:TC], S2[:, 0:TC], ALU.add)
                    inc(i, s_v)
                    p.wait(e, "vector", s_v, s_v.n)
                    sg(p, e, "vector", s_v,
                       e.reduce_sum(S3[:, 0:1], DC[:, o * TC:(o + 1) * TC],
                                    axis=AX.X))
                    sg(p, e, "vector", s_v,
                       e.tensor_tensor(sumd[:, o:o + 1], sumd[:, o:o + 1],
                                       S3[:, 0:1], ALU.add))

            dcv = s_v.n

            @body(blk, "sync")
            def _(e):
                p.wait(e, "sync", s_v, dcv)
                for g in range(EDT):
                    e.dma_start(out=o_dl[g * 128:(g + 1) * 128, tsl],
                                in_=DC[:, g * TC:(g + 1) * TC]
                                ).then_inc(s_st.h, 16)
                e.dma_start(out=o_c[0:16, tsl],
                            in_=DBC[80:96, :]).then_inc(s_st.h, 16)
                s_st.n += (EDT + 1) * 16
                e.wait_ge(s_st.h, s_st.n)

            bct0 = s_t.n
            bc_v0 = s_v.n

            @body(blk, "tensor")
            def _(e):
                p.wait(e, "tensor", s_v, dbcv)
                p.wait(e, "tensor", s_a, s_a.n)
                for n in range(2 * NST):
                    if n >= 3:
                        p.wait(e, "tensor", s_v, bc_v0 + n - 2)
                    i = e.matmul(psB[n % 3][:, :],
                                 sel32[:, n * 128:(n + 1) * 128],
                                 BCR[0:32, :], start=True, stop=True)
                    inc(i, s_t)

            @body(blk, "vector")
            def _(e):
                for n in range(2 * NST):
                    p.wait(e, "vector", s_t, bct0 + n + 1)
                    i = e.tensor_copy(BBC[:, n * TC:(n + 1) * TC],
                                      psB[n % 3][:, :])
                    inc(i, s_v)

            @body(blk, "gpsimd")
            def _(e):
                pass

            for eb in range(EDT):
                esl = slice(eb * TC, (eb + 1) * TC)

                @body(blk, "vector")
                def _(e):
                    p.wait(e, "vector", s_v, s_v.n)
                    p.wait(e, "vector", s_g, s_g.n)
                    i = e.tensor_tensor(S1[:, 0:TC], DC[:, esl], UC[:, esl],
                                        ALU.mult)
                    inc(i, s_v)
                duv = s_v.n

                for n in range(NST):
                    lane = eb * NST + n

                    @body(blk, "scalar")
                    def _(e):
                        p.wait(e, "scalar", s_v, dcv)
                        if getattr(p, "_sc_ack", 0):
                            p.wait(e, "scalar", s_v, p._sc_ack)
                        i = e.activation(S2[:, 0:TC], DC[:, esl], AF.Exp,
                                         scale=acol[:, lane:lane + 1])
                        inc(i, s_a)

                    @body(blk, "gpsimd")
                    def _(e):
                        p.wait(e, "gpsimd", s_v, duv)
                        p.wait(e, "gpsimd", s_ld, s_ld.n)
                        if getattr(p, "_sc_ack", 0):
                            p.wait(e, "gpsimd", s_v, p._sc_ack)
                        i = e.tensor_tensor(S3[:, 0:TC], S1[:, 0:TC],
                                            BBC[:, n * TC:(n + 1) * TC],
                                            ALU.mult)
                        inc(i, s_g)

                    @body(blk, "vector")
                    def _(e):
                        p.wait(e, "vector", s_v, s_v.n)
                        p.wait(e, "vector", s_a, s_a.n)
                        p.wait(e, "vector", s_g, s_g.n)
                        init = (0.0 if tc == 0 else
                                carry[:, lane:lane + 1])
                        i = e.tensor_tensor_scan(
                            HB[:, n * TC:(n + 1) * TC], S2[:, 0:TC],
                            S3[:, 0:TC], init, ALU.mult, ALU.add)
                        inc(i, s_v)
                        p.wait(e, "vector", s_v, s_v.n)
                        i = e.tensor_copy(carry[:, lane:lane + 1],
                                          HB[:, (n + 1) * TC - 1:
                                             (n + 1) * TC])
                        inc(i, s_v)
                        p.wait(e, "vector", s_v, s_v.n)
                        i = e.tensor_tensor(HB[:, n * TC:(n + 1) * TC],
                                            HB[:, n * TC:(n + 1) * TC],
                                            BBC[:, (NST + n) * TC:
                                                (NST + n + 1) * TC],
                                            ALU.mult)
                        inc(i, s_v)
                        p._sc_ack = s_v.n

                @body(blk, "vector")
                def _(e):
                    p.wait(e, "vector", s_v, s_v.n)
                    hv = bass.AP(HB, 0, [[EDT * 528, 128], [1, TC],
                                         [TC, NST]])
                    i = e.tensor_reduce(YC[:, esl], hv, axis=AX.X,
                                        op=ALU.add)
                    inc(i, s_v)

            @body(blk, "sync")
            def _(e):
                p.wait(e, "sync", s_v, s_v.n)
                for g in range(EDT):
                    e.dma_start(out=o_yl[g * 128:(g + 1) * 128, tsl],
                                in_=YC[:, g * TC:(g + 1) * TC]
                                ).then_inc(s_st.h, 16)
                s_st.n += EDT * 16
                e.wait_ge(s_st.h, s_st.n)

        @body(blk, "sync")
        def _(e):
            p.wait(e, "sync", s_v, s_v.n)
            e.dma_start(out=o_he, in_=carry[:]).then_inc(s_st.h, 16)
            e.dma_start(out=o_sd, in_=sumd[:]).then_inc(s_st.h, 16)
            s_st.n += 32
            e.wait_ge(s_st.h, s_st.n)

    return p




def build_MB(l):
    p = Pr(f"MB{l}")
    yl_d = p.dram_in("ylocal", [ED, TL])
    dl_d = p.dram_in("delta", [ED, TL])
    u_d = p.dram_in("u_bf", [ED, TL], BF16)
    cr_d = p.dram_in("c_rows", [16, TL])
    z_d = p.dram_in("zin", [ED, TL], BF16)
    xb_d = p.dram_in("xbase", [D, TL])
    hg_d = p.dram_in("hendm_g", [128, 3 * EDT * NST])
    sg_d = p.dram_in("sumd_g", [128, 3 * EDT])
    beta_d = p.dram_in("beta", [128, 3])
    gamma_d = p.dram_in("gamma", [128, 3])
    alog_d = p.dram_in(f"m{l}_Alog", [128, EDT * NST])
    dp_d = p.dram_in(f"m{l}_Dp", [128, EDT])
    op_d = p.dram_in(f"m{l}_outproj", [ED, D])
    sel16_d = p.dram_in("sel16", [16, 16 * 128], BF16)
    o_hn = p.dram_out("hnext", [D, TL])
    if l == 0:
        nw_d = p.dram_in("m1_normw", [128, DT])
        ip_d = p.dram_in("m1_inproj", [D, 2 * ED])
        o_xi = p.dram_out("xi1", [ED, TL])
        o_z = p.dram_out("z1", [ED, TL], BF16)
    else:
        x1_d = p.dram_in("x1", [D, TL])
        o_xp = p.dram_out("xp", [D, TL])

    with p.ctx:
        DCt = p.sbuf("DCt", [128, TC])
        YCt = p.sbuf("YCt", [128, TC])
        ZB = p.sbuf("ZB", [128, EDT * TC], BF16)
        CB = p.sbuf("CB", [128, NST * TC], BF16)
        WOP = p.sbuf("WOP", [128, EDT * D], BF16)
        HN = p.sbuf("HN", [128, DT * TC])
        S1 = p.sbuf("S1", [128, TC])
        S2 = p.sbuf("S2", [128, TC])
        S3 = p.sbuf("S3", [128, TC])
        S4 = p.sbuf("S4", [128, TC])
        S5 = p.sbuf("S5", [128, TC])
        UBt = p.sbuf("UBt", [128, TC], BF16)
        CR = p.sbuf("CR", [16, TC])
        CRB = p.sbuf("CRB", [16, TC], BF16)
        sel16 = p.sbuf("sel16_s", [16, 16 * 128], BF16)
        acol = p.sbuf("acol", [128, EDT * NST])
        hin = p.sbuf("hin", [128, EDT * NST])
        lgp = p.sbuf("lgp", [128, EDT * NST])
        pfb = p.sbuf("pfb", [128, EDT * NST])
        hgt = p.sbuf("hgt", [128, 3 * EDT * NST])
        sgt = p.sbuf("sgt", [128, 3 * EDT])
        bet = p.sbuf("bet", [128, 3])
        gam = p.sbuf("gam", [128, 3])
        carryD = p.sbuf("carryD", [128, EDT])
        dpc = p.sbuf("dpc", [128, EDT])
        rows = p.sbuf("rows", [1, 2 * TC])
        ones_b = p.sbuf("ones_b", [128, 1], BF16)
        ones_f1 = p.sbuf("ones_f1", [1, 128])
        if l == 0:
            nwc = p.sbuf("nwc", [128, DT])
            BQ = p.sbuf("BQ", [128, DT * TC], BF16)
            wb = [p.sbuf(f"wb{i}", [128, DT * 1024], BF16) for i in range(2)]
            stgb = p.sbuf("stgb", [128, DT * TC], BF16)
        else:
            XR = p.sbuf("XR", [128, DT * TC])

        psA = [p.psum(f"psA{i}", [128, TC]) for i in range(3)]
        psB = [p.psum(f"psB{i}", [128, TC]) for i in range(3)]
        psSt = p.psum("psSt", [1, TC])
        psBc = p.psum("psBc", [128, TC])

        s_ld = p.sem("s_ld")
        s_w = p.sem("s_w")
        s_t = p.sem("s_t")
        s_v = p.sem("s_v")
        s_a = p.sem("s_a")
        s_st = p.sem("s_st")
        blk = p.open_block()
        p._mb_ack = 0

        @body(blk, "sync")
        def _(e):
            e.dma_start(out=hgt[:], in_=hg_d).then_inc(s_ld.h, 16)
            e.dma_start(out=sgt[:], in_=sg_d).then_inc(s_ld.h, 16)
            e.dma_start(out=bet[:], in_=beta_d).then_inc(s_ld.h, 16)
            e.dma_start(out=gam[:], in_=gamma_d).then_inc(s_ld.h, 16)
            e.dma_start(out=acol[:], in_=alog_d).then_inc(s_ld.h, 16)
            e.dma_start(out=dpc[:], in_=dp_d).then_inc(s_ld.h, 16)
            s_ld.n += 6 * 16
            if l == 0:
                e.dma_start(out=nwc[:], in_=nw_d).then_inc(s_ld.h, 16)
                s_ld.n += 16
            e.wait_ge(s_ld.h, s_ld.n)

        @body(blk, "gpsimd")
        def _(e):
            e.dma_start(out=sel16[:], in_=sel16_d).then_inc(s_w.h, 16)
            s_w.n += 16
            e.wait_ge(s_w.h, s_w.n)
            e.dma_start(
                out=bass.AP(WOP, 0, [[EDT * D, 128], [D, EDT], [1, D]]),
                in_=op_d.rearrange("(k p) o -> p k o", p=128)
            ).then_inc(s_w.h, 16)
            s_w.n += 16
            e.wait_ge(s_w.h, s_w.n)

        @body(blk, "scalar")
        def _(e):
            p.wait(e, "scalar", s_ld, s_ld.n)
            i = e.activation(lgp[:], acol[:], AF.Exp)
            inc(i, s_a)

        @body(blk, "vector")
        def _(e):
            p.wait(e, "vector", s_a, s_a.n)
            sg(p, e, "vector", s_v,
               e.tensor_scalar(acol[:], lgp[:], -1.0, None, ALU.mult))
            e.memset(carryD[:], 0.0)
            e.memset(ones_f1[:], 1.0)
            e.memset(ones_b[:], 1.0)
            sg(p, e, "vector", s_v, e.memset(hin[:], 0.0))

        for j in range(3):
            @body(blk, "vector")
            def _(e, j=j):
                p.wait(e, "vector", s_v, s_v.n)
                for eb in range(EDT):
                    i = e.tensor_scalar(
                        lgp[:, eb * NST:(eb + 1) * NST],
                        acol[:, eb * NST:(eb + 1) * NST],
                        sgt[:, j * EDT + eb:j * EDT + eb + 1], None,
                        ALU.mult)
                    inc(i, s_v)
                p.wait(e, "vector", s_v, s_v.n)
                i = e.tensor_scalar(lgp[:], lgp[:], gam[:, j:j + 1], None,
                                    ALU.mult)
                inc(i, s_v)

            @body(blk, "scalar")
            def _(e):
                p.wait(e, "scalar", s_v, s_v.n)
                i = e.activation(pfb[:], lgp[:], AF.Exp)
                inc(i, s_a)

            @body(blk, "vector")
            def _(e, j=j):
                p.wait(e, "vector", s_a, s_a.n)
                sg(p, e, "vector", s_v,
                   e.tensor_tensor(hin[:], hin[:], pfb[:], ALU.mult))
                sg(p, e, "vector", s_v,
                   e.tensor_scalar(lgp[:],
                                   hgt[:, j * EDT * NST:(j + 1) * EDT * NST],
                                   bet[:, j:j + 1], None, ALU.mult))
                sg(p, e, "vector", s_v,
                   e.tensor_tensor(hin[:], hin[:], lgp[:], ALU.add))

        wload_t = [0, 0]

        for tc in range(NT):
            tsl = slice(tc * TC, (tc + 1) * TC)

            @body(blk, "sync")
            def _(e):
                p.wait(e, "sync", s_v, s_v.n)
                p.wait(e, "sync", s_a, s_a.n)
                p.wait(e, "sync", s_st, s_st.n)
                e.dma_start(out=CR[0:16, :], in_=cr_d[0:16, tsl]
                            ).then_inc(s_ld.h, 16)
                s_ld.n += 16
                e.wait_ge(s_ld.h, s_ld.n)

            @body(blk, "gpsimd")
            def _(e):
                p.wait(e, "gpsimd", s_a, s_a.n)
                p.wait(e, "gpsimd", s_t, s_t.n)
                for g in range(EDT):
                    e.dma_start(out=ZB[:, g * TC:(g + 1) * TC],
                                in_=z_d[g * 128:(g + 1) * 128, tsl]
                                ).then_inc(s_w.h, 16)
                s_w.n += EDT * 16
                e.wait_ge(s_w.h, s_w.n)
            zld = s_w.n

            @body(blk, "vector")
            def _(e):
                p.wait(e, "vector", s_ld, s_ld.n)
                i = e.tensor_copy(CRB[0:16, :], CR[0:16, :])
                inc(i, s_v)
            crbv = s_v.n

            bct0 = s_t.n
            bc_v0 = s_v.n

            @body(blk, "tensor")
            def _(e):
                p.wait(e, "tensor", s_v, crbv)
                p.wait(e, "tensor", s_w, 32)
                for n in range(NST):
                    if n >= 3:
                        p.wait(e, "tensor", s_v, bc_v0 + n - 2)
                    i = e.matmul(psB[n % 3][:, :],
                                 sel16[:, n * 128:(n + 1) * 128],
                                 CRB[0:16, :], start=True, stop=True)
                    inc(i, s_t)

            @body(blk, "vector")
            def _(e):
                for n in range(NST):
                    p.wait(e, "vector", s_t, bct0 + n + 1)
                    i = e.tensor_copy(CB[:, n * TC:(n + 1) * TC],
                                      psB[n % 3][:, :])
                    inc(i, s_v)

            for eb in range(EDT):
                esl = slice(eb * TC, (eb + 1) * TC)
                lane0 = eb * NST

                @body(blk, "sync")
                def _(e, eb=eb):
                    p.wait(e, "sync", s_v, s_v.n)
                    e.dma_start(out=UBt[:, :],
                                in_=u_d[eb * 128:(eb + 1) * 128, tsl]
                                ).then_inc(s_ld.h, 16)
                    e.dma_start(out=DCt[:, :],
                                in_=dl_d[eb * 128:(eb + 1) * 128, tsl]
                                ).then_inc(s_ld.h, 16)
                    e.dma_start(out=YCt[:, :],
                                in_=yl_d[eb * 128:(eb + 1) * 128, tsl]
                                ).then_inc(s_ld.h, 16)
                    s_ld.n += 3 * 16
                    e.wait_ge(s_ld.h, s_ld.n)

                @body(blk, "vector")
                def _(e, eb=eb, tc=tc):
                    p.wait(e, "vector", s_v, s_v.n)
                    p.wait(e, "vector", s_a, s_a.n)
                    p.wait(e, "vector", s_ld, s_ld.n)
                    init = 0.0 if tc == 0 else carryD[:, eb:eb + 1]
                    sg(p, e, "vector", s_v,
                       e.tensor_tensor_scan(S4[:, :], DCt[:, :], DCt[:, :],
                                            init, ALU.add, ALU.bypass))
                    sg(p, e, "vector", s_v,
                       e.tensor_copy(carryD[:, eb:eb + 1], S4[:, TC - 1:TC]))
                    i = e.memset(S5[:, :], 0.0)
                    inc(i, s_v)
                cdv = s_v.n

                for n in range(NST):
                    lane = lane0 + n

                    @body(blk, "scalar")
                    def _(e, lane=lane, cdv=cdv):
                        p.wait(e, "scalar", s_v, cdv)
                        p.wait(e, "scalar", s_v, p._mb_ack)
                        i = e.activation(S2[:, :], S4[:, :], AF.Exp,
                                         scale=acol[:, lane:lane + 1])
                        inc(i, s_a)

                    @body(blk, "vector")
                    def _(e, lane=lane, n=n):
                        p.wait(e, "vector", s_a, s_a.n)
                        sg(p, e, "vector", s_v,
                           e.scalar_tensor_tensor(
                               S3[:, :], S2[:, :], hin[:, lane:lane + 1],
                               CB[:, n * TC:(n + 1) * TC],
                               ALU.mult, ALU.mult))
                        i = e.tensor_tensor(S5[:, :], S5[:, :], S3[:, :],
                                            ALU.add)
                        inc(i, s_v)
                        p._mb_ack = s_v.n

                @body(blk, "vector")
                def _(e, eb=eb):
                    p.wait(e, "vector", s_v, s_v.n)
                    sg(p, e, "vector", s_v,
                       e.tensor_tensor(S1[:, :], YCt[:, :], S5[:, :],
                                       ALU.add))
                    p.wait(e, "vector", s_ld, s_ld.n)
                    sg(p, e, "vector", s_v,
                       e.scalar_tensor_tensor(S1[:, :], UBt[:, :],
                                              dpc[:, eb:eb + 1], S1[:, :],
                                              ALU.mult, ALU.add))

                @body(blk, "scalar")
                def _(e, eb=eb, zld=zld):
                    p.wait(e, "scalar", s_w, zld)
                    p.wait(e, "scalar", s_v, p._mb_ack)
                    i = e.activation(S2[:, :], ZB[:, esl], AF.Silu)
                    inc(i, s_a)

                @body(blk, "vector")
                def _(e, eb=eb):
                    p.wait(e, "vector", s_a, s_a.n)
                    p.wait(e, "vector", s_v, s_v.n)
                    i = e.tensor_tensor(ZB[:, esl], S1[:, :], S2[:, :],
                                        ALU.mult)
                    inc(i, s_v)
            ybv = s_v.n

            @body(blk, "sync")
            def _(e):
                p.wait(e, "sync", s_v, s_v.n)
                p.wait(e, "sync", s_st, s_st.n)
                for d in range(DT):
                    e.dma_start(out=HN[:, d * TC:(d + 1) * TC],
                                in_=xb_d[d * 128:(d + 1) * 128, tsl]
                                ).then_inc(s_ld.h, 16)
                s_ld.n += DT * 16
                e.wait_ge(s_ld.h, s_ld.n)

            def evac_hn(e, o, ps):
                return e.tensor_tensor(HN[:, o * TC:(o + 1) * TC], ps[:, :],
                                       HN[:, o * TC:(o + 1) * TC], ALU.add)

            mm(p, s_t, psA, EDT, DT,
               lambda k, o: WOP[:, k * D + o * 128:k * D + o * 128 + 128],
               lambda k: ZB[:, k * TC:(k + 1) * TC],
               [("vector", s_v, evac_hn)],
               tdeps=[(s_v, ybv), (s_w, 32), (s_st, s_st.n),
                      (s_ld, s_ld.n)],
               edeps=[(s_ld, s_ld.n)])
            hnv = s_v.n

            @body(blk, "sync")
            def _(e):
                p.wait(e, "sync", s_v, hnv)
                for d in range(DT):
                    e.dma_start(out=o_hn[d * 128:(d + 1) * 128, tsl],
                                in_=HN[:, d * TC:(d + 1) * TC]
                                ).then_inc(s_st.h, 16)
                s_st.n += DT * 16
                e.wait_ge(s_st.h, s_st.n)

            if l == 1:
                @body(blk, "sync")
                def _(e):
                    p.wait(e, "sync", s_st, s_st.n)
                    for d in range(DT):
                        e.dma_start(out=XR[:, d * TC:(d + 1) * TC],
                                    in_=x1_d[d * 128:(d + 1) * 128, tsl]
                                    ).then_inc(s_ld.h, 16)
                    s_ld.n += DT * 16
                    e.wait_ge(s_ld.h, s_ld.n)

                @body(blk, "vector")
                def _(e):
                    p.wait(e, "vector", s_ld, s_ld.n)
                    for d in range(DT):
                        dsl = slice(d * TC, (d + 1) * TC)
                        i = e.tensor_tensor(XR[:, dsl], XR[:, dsl],
                                            HN[:, dsl], ALU.add)
                        inc(i, s_v)

                @body(blk, "sync")
                def _(e):
                    p.wait(e, "sync", s_v, s_v.n)
                    for d in range(DT):
                        e.dma_start(out=o_xp[d * 128:(d + 1) * 128, tsl],
                                    in_=XR[:, d * TC:(d + 1) * TC]
                                    ).then_inc(s_st.h, 16)
                    s_st.n += DT * 16
                    e.wait_ge(s_st.h, s_st.n)
            else:
                @body(blk, "vector")
                def _(e):
                    p.wait(e, "vector", s_v, s_v.n)
                    p.wait(e, "vector", s_t, s_t.n)
                    for d in range(DT):
                        dsl = slice(d * TC, (d + 1) * TC)
                        i = e.tensor_tensor(BQ[:, dsl], HN[:, dsl],
                                            HN[:, dsl], ALU.mult)
                        inc(i, s_v)

                @body(blk, "tensor")
                def _(e):
                    p.wait(e, "tensor", s_v, s_v.n)
                    for d in range(DT):
                        i = e.matmul(psSt[0:1, :], ones_b[:],
                                     BQ[:, d * TC:(d + 1) * TC],
                                     start=(d == 0), stop=(d == DT - 1))
                    inc(i, s_t)

                @body(blk, "vector")
                def _(e):
                    p.wait(e, "vector", s_t, s_t.n)
                    r1 = rows[0:1, 0:TC]
                    sg(p, e, "vector", s_v,
                       e.tensor_scalar(r1, psSt[0:1, :], 1.0 / D, None,
                                       ALU.mult))
                    i = e.tensor_scalar(r1, r1, 1e-5, None, ALU.add)
                    inc(i, s_v)

                @body(blk, "scalar")
                def _(e):
                    p.wait(e, "scalar", s_v, s_v.n)
                    r1 = rows[0:1, 0:TC]
                    r2 = rows[0:1, TC:2 * TC]
                    sg(p, e, "scalar", s_a, e.activation(r2, r1, AF.Ln))
                    i = e.activation(r1, r2, AF.Exp, scale=-0.5)
                    inc(i, s_a)

                @body(blk, "tensor")
                def _(e):
                    p.wait(e, "tensor", s_a, s_a.n)
                    i = e.matmul(psBc[:, :], ones_f1[:], rows[0:1, 0:TC],
                                 start=True, stop=True)
                    inc(i, s_t)

                @body(blk, "vector")
                def _(e):
                    p.wait(e, "vector", s_t, s_t.n)
                    for d in range(DT):
                        dsl = slice(d * TC, (d + 1) * TC)
                        sg(p, e, "vector", s_v,
                           e.tensor_tensor(S1[:, :], HN[:, dsl], psBc[:, :],
                                           ALU.mult))
                        sg(p, e, "vector", s_v,
                           e.tensor_scalar(BQ[:, dsl], S1[:, :],
                                           nwc[:, d:d + 1], None, ALU.mult))

                for qq in range(4):
                    widx = qq % 2

                    @body(blk, "gpsimd")
                    def _(e, widx=widx, qq=qq):
                        p.wait(e, "gpsimd", s_t, wload_t[widx])
                        e.dma_start(
                            out=bass.AP(wb[widx], 0,
                                        [[DT * 1024, 128], [1024, DT],
                                         [1, 1024]]),
                            in_=ip_d.rearrange("(k p) o -> p k o", p=128)[
                                :, :, qq * 1024:(qq + 1) * 1024]
                        ).then_inc(s_w.h, 16)
                        s_w.n += 16
                        e.wait_ge(s_w.h, s_w.n)
                    wq = s_w.n

                    def evac_ip(e, o, ps, qq=qq):
                        if qq < 2:
                            return e.tensor_copy(HN[:, o * TC:(o + 1) * TC],
                                                 ps[:, :])
                        return e.tensor_copy(stgb[:, o * TC:(o + 1) * TC],
                                             ps[:, :])

                    st_v0 = s_v.n
                    mm(p, s_t, psA, DT, DT,
                       lambda k, o, widx=widx: wb[widx][
                           :, k * 1024 + o * 128:k * 1024 + o * 128 + 128],
                       lambda k: BQ[:, k * TC:(k + 1) * TC],
                       [("vector", s_v, evac_ip)],
                       tdeps=[(s_v, s_v.n), (s_w, wq), (s_st, s_st.n)])
                    wload_t[widx] = s_t.n

                    @body(blk, "sync")
                    def _(e, qq=qq, st_v0=st_v0):
                        for o in range(DT):
                            go = qq * 8 + o
                            p.wait(e, "sync", s_v, st_v0 + o + 1)
                            if qq < 2:
                                e.dma_start(
                                    out=o_xi[go * 128:(go + 1) * 128, tsl],
                                    in_=HN[:, o * TC:(o + 1) * TC]
                                    ).then_inc(s_st.h, 16)
                            else:
                                e.dma_start(
                                    out=o_z[(go - 16) * 128:
                                            (go - 15) * 128, tsl],
                                    in_=stgb[:, o * TC:(o + 1) * TC]
                                    ).then_inc(s_st.h, 16)
                            s_st.n += 16
                            e.wait_ge(s_st.h, s_st.n)

        @body(blk, "sync")
        def _(e):
            e.wait_ge(s_st.h, s_st.n)

    return p



def build_G(final=False):
    p = Pr("G5" if final else "G")
    xg_d = p.dram_in("xg", [D, TL])
    xgh_d = p.dram_in("xgh", [D, 1])
    w1a_d = p.dram_in("g_W1a", [D, FFP])
    w1g_d = p.dram_in("g_W1g", [D, FFP])
    w2_d = p.dram_in("g_W2", [FFP, D])
    cols_d = {nm: p.dram_in("g_" + nm, [128, DT])
              for nm in ["lng", "lnb", "b2"]}
    b1a_d = p.dram_in("g_b1a", [128, FFT])
    b1g_d = p.dram_in("g_b1g", [128, FFT])
    if final:
        xp_d = p.dram_in("xpf", [D, TL])
    o_x = p.dram_out("xout", [D, TL])

    with p.ctx:
        WA = p.sbuf("WA", [128, DT * FFP], BF16)
        WG = p.sbuf("WG", [128, DT * FFP], BF16)
        W2B = p.sbuf("W2B", [128, FFT * D], BF16)
        XC = p.sbuf("XC", [128, DT * 513])
        BQ = p.sbuf("BQ", [128, DT * TC], BF16)
        ABH = p.sbuf("ABH", [128, 3 * TC], BF16)
        HB = p.sbuf("HB", [128, FFT * TC], BF16)
        SC2 = p.sbuf("SC2", [128, 3 * TC])
        SC = p.sbuf("SC", [128, TC])
        rows = p.sbuf("rows", [1, 3 * TC])
        cols = p.sbuf("colsg", [128, 3 * DT])
        b1ac = p.sbuf("b1ac", [128, FFT])
        b1gc = p.sbuf("b1gc", [128, FFT])
        ones_b = p.sbuf("ones_b", [128, 1], BF16)
        ones_f1 = p.sbuf("ones_f1", [1, 128])
        ones_f = p.sbuf("ones_f", [128, 1])
        if final:
            XPT = p.sbuf("XPT", [128, 2 * TC])

        psA = [p.psum(f"psA{i}", [128, TC]) for i in range(3)]
        psSt = p.psum("psSt", [1, TC])
        psSt2 = p.psum("psSt2", [1, TC])
        psBc = p.psum("psBc", [128, TC])
        psBc2 = p.psum("psBc2", [128, TC])

        s_ld = p.sem("s_ld")
        s_w = p.sem("s_w")
        s_t = p.sem("s_t")
        s_v = p.sem("s_v")
        s_a = p.sem("s_a")
        s_st = p.sem("s_st")
        blk = p.open_block()
        p._g_hbv = {}

        @body(blk, "sync")
        def _(e):
            for j, nm in enumerate(["lng", "lnb", "b2"]):
                e.dma_start(out=cols[:, j * DT:(j + 1) * DT],
                            in_=cols_d[nm]).then_inc(s_ld.h, 16)
            e.dma_start(out=b1ac[:], in_=b1a_d).then_inc(s_ld.h, 16)
            e.dma_start(out=b1gc[:], in_=b1g_d).then_inc(s_ld.h, 16)
            s_ld.n += 5 * 16
            e.wait_ge(s_ld.h, s_ld.n)

        @body(blk, "gpsimd")
        def _(e):
            for dst, dram, kk, w in [(WA, w1a_d, DT, FFP),
                                     (WG, w1g_d, DT, FFP),
                                     (W2B, w2_d, FFT, D)]:
                e.dma_start(
                    out=bass.AP(dst, 0, [[kk * w, 128], [w, kk], [1, w]]),
                    in_=dram.rearrange("(k p) o -> p k o", p=128)
                ).then_inc(s_w.h, 16)
                s_w.n += 16
                e.wait_ge(s_w.h, s_w.n)

        @body(blk, "vector")
        def _(e):
            e.memset(ones_b[:], 1.0)
            e.memset(ones_f1[:], 1.0)
            i = e.memset(ones_f[:], 1.0)
            inc(i, s_v)

        def xs(d):
            if d < 4:
                return XC[:, d * 513 + 1:d * 513 + 513]
            return XC[:, d * 513:d * 513 + 512]

        for tc in range(NT):
            tsl = slice(tc * TC, (tc + 1) * TC)

            @body(blk, "sync")
            def _(e, tc=tc):
                p.wait(e, "sync", s_v, s_v.n)
                p.wait(e, "sync", s_st, s_st.n)
                for d in range(DT):
                    if tc == 0:
                        e.dma_start(out=XC[:, d * 513:d * 513 + 1],
                                    in_=xgh_d[d * 128:(d + 1) * 128, 0:1]
                                    ).then_inc(s_ld.h, 16)
                        e.dma_start(out=XC[:, d * 513 + 1:d * 513 + 513],
                                    in_=xg_d[d * 128:(d + 1) * 128, 0:TC]
                                    ).then_inc(s_ld.h, 16)
                        s_ld.n += 32
                    else:
                        e.dma_start(
                            out=XC[:, d * 513:d * 513 + 513],
                            in_=xg_d[d * 128:(d + 1) * 128,
                                     tc * TC - 1:(tc + 1) * TC]
                            ).then_inc(s_ld.h, 16)
                        s_ld.n += 16
                e.wait_ge(s_ld.h, s_ld.n)

            @body(blk, "vector")
            def _(e):
                p.wait(e, "vector", s_ld, s_ld.n)
                p.wait(e, "vector", s_t, s_t.n)
                for d in range(DT):
                    i = e.tensor_tensor(BQ[:, d * TC:(d + 1) * TC],
                                        xs(d), xs(d), ALU.mult)
                    inc(i, s_v)
            sqv = s_v.n

            @body(blk, "tensor")
            def _(e):
                p.wait(e, "tensor", s_v, sqv)
                for d in range(DT):
                    i = e.matmul(psSt[0:1, :], ones_f[:], xs(d),
                                 start=(d == 0), stop=(d == DT - 1))
                inc(i, s_t)
                for d in range(DT):
                    i = e.matmul(psSt2[0:1, :], ones_b[:],
                                 BQ[:, d * TC:(d + 1) * TC],
                                 start=(d == 0), stop=(d == DT - 1))
                inc(i, s_t)

            @body(blk, "vector")
            def _(e):
                p.wait(e, "vector", s_t, s_t.n)
                r0 = rows[0:1, 0:TC]
                r1 = rows[0:1, TC:2 * TC]
                r2 = rows[0:1, 2 * TC:3 * TC]
                sg(p, e, "vector", s_v,
                   e.tensor_scalar(r0, psSt[0:1, :], 1.0 / D, None, ALU.mult))
                sg(p, e, "vector", s_v,
                   e.tensor_scalar(r1, psSt2[0:1, :], 1.0 / D, None,
                                   ALU.mult))
                sg(p, e, "vector", s_v,
                   e.tensor_tensor(r2, r0, r0, ALU.mult))
                sg(p, e, "vector", s_v,
                   e.tensor_tensor(r1, r1, r2, ALU.subtract))
                sg(p, e, "vector", s_v,
                   e.tensor_scalar(r1, r1, 1e-5, None, ALU.add))

            @body(blk, "scalar")
            def _(e):
                p.wait(e, "scalar", s_v, s_v.n)
                r1 = rows[0:1, TC:2 * TC]
                r2 = rows[0:1, 2 * TC:3 * TC]
                sg(p, e, "scalar", s_a, e.activation(r2, r1, AF.Ln))
                i = e.activation(r1, r2, AF.Exp, scale=-0.5)
                inc(i, s_a)

            @body(blk, "tensor")
            def _(e):
                p.wait(e, "tensor", s_a, s_a.n)
                i = e.matmul(psBc[:, :], ones_f1[:], rows[0:1, 0:TC],
                             start=True, stop=True)
                inc(i, s_t)
                i = e.matmul(psBc2[:, :], ones_f1[:], rows[0:1, TC:2 * TC],
                             start=True, stop=True)
                inc(i, s_t)

            @body(blk, "vector")
            def _(e):
                p.wait(e, "vector", s_t, s_t.n)
                for d in range(DT):
                    sg(p, e, "vector", s_v,
                       e.tensor_tensor(SC[:, :], xs(d), psBc[:, :],
                                       ALU.subtract))
                    sg(p, e, "vector", s_v,
                       e.tensor_tensor(SC[:, :], SC[:, :], psBc2[:, :],
                                       ALU.mult))
                    sg(p, e, "vector", s_v,
                       e.tensor_scalar(SC[:, :], SC[:, :],
                                       cols[:, d:d + 1], None, ALU.mult))
                    sg(p, e, "vector", s_v,
                       e.tensor_scalar(BQ[:, d * TC:(d + 1) * TC], SC[:, :],
                                       cols[:, DT + d:DT + d + 1], None,
                                       ALU.add))

            base_t = s_t.n
            relv = {}
            rela = {}
            bqv = s_v.n

            for j in range(2 * FFT):
                o = j // 2

                @body(blk, "tensor")
                def _(e, j=j, o=o):
                    if j == 0:
                        p.wait(e, "tensor", s_v, bqv)
                        p.wait(e, "tensor", s_w, 48)
                    if j >= 3:
                        jj = j - 3
                        if jj % 2 == 0:
                            p.wait(e, "tensor", s_v, relv[jj])
                        else:
                            p.wait(e, "tensor", s_a, rela[jj])
                    wb = WA if j % 2 == 0 else WG
                    for k in range(DT):
                        i = e.matmul(psA[j % 3][:, :],
                                     wb[:, k * FFP + o * 128:
                                        k * FFP + o * 128 + 128],
                                     BQ[:, k * TC:(k + 1) * TC],
                                     start=(k == 0), stop=(k == DT - 1))
                    inc(i, s_t)

                if j % 2 == 0:
                    @body(blk, "vector")
                    def _(e, j=j, o=o):
                        p.wait(e, "vector", s_t, base_t + j + 1)
                        i = e.tensor_scalar(ABH[:, (o % 3) * TC:
                                                (o % 3 + 1) * TC],
                                            psA[j % 3][:, :],
                                            b1ac[:, o:o + 1], None, ALU.add)
                        inc(i, s_v)
                        relv[j] = s_v.n
                else:
                    @body(blk, "scalar")
                    def _(e, j=j, o=o):
                        p.wait(e, "scalar", s_t, base_t + j + 1)
                        prevo = o - 3 if o >= 3 else FFT - 3 + o
                        p.wait(e, "scalar", s_v, p._g_hbv.get(prevo, 0))
                        i = e.activation(SC2[:, (o % 3) * TC:(o % 3 + 1) * TC],
                                         psA[j % 3][:, :], AF.Gelu,
                                         bias=b1gc[:, o:o + 1])
                        inc(i, s_a)
                        rela[j] = s_a.n

                    @body(blk, "vector")
                    def _(e, j=j, o=o):
                        p.wait(e, "vector", s_a, rela[j])
                        p.wait(e, "vector", s_v, relv[j - 1])
                        i = e.tensor_tensor(HB[:, o * TC:(o + 1) * TC],
                                            SC2[:, (o % 3) * TC:
                                                (o % 3 + 1) * TC],
                                            ABH[:, (o % 3) * TC:
                                                (o % 3 + 1) * TC],
                                            ALU.mult)
                        inc(i, s_v)
                        p._g_hbv[o] = s_v.n
            hbv = s_v.n

            ipt = 3 if final else 2
            vbase = s_v.n
            xpld = []
            if final:
                @body(blk, "sync")
                def _(e, tc=tc, vbase=vbase):
                    for o in range(DT):
                        if o >= 2:
                            p.wait(e, "sync", s_v, vbase + 3 * (o - 1))
                        e.dma_start(out=XPT[:, (o % 2) * TC:
                                            (o % 2 + 1) * TC],
                                    in_=xp_d[o * 128:(o + 1) * 128, tsl]
                                    ).then_inc(s_ld.h, 16)
                        s_ld.n += 16
                        e.wait_ge(s_ld.h, s_ld.n)
                        xpld.append(s_ld.n)

            def evac_x(e, o, ps):
                p.wait(e, "vector", s_v, s_v.n)
                sg(p, e, "vector", s_v,
                   e.tensor_scalar(SC[:, :], ps[:, :],
                                   cols[:, 2 * DT + o:2 * DT + o + 1],
                                   None, ALU.add))
                x_o = XC[:, o * 513 + 1:o * 513 + 513]
                if final:
                    p.wait(e, "vector", s_ld, xpld[o])
                    sg(p, e, "vector", s_v,
                       e.tensor_tensor(SC[:, :], SC[:, :],
                                       XPT[:, (o % 2) * TC:(o % 2 + 1) * TC],
                                       ALU.add))
                return e.tensor_tensor(x_o, x_o, SC[:, :], ALU.add)

            mm(p, s_t, psA, FFT, DT,
               lambda k, o: W2B[:, k * D + o * 128:k * D + o * 128 + 128],
               lambda k: HB[:, k * TC:(k + 1) * TC],
               [("vector", s_v, evac_x, ipt)],
               tdeps=[(s_v, hbv), (s_a, s_a.n)])

            @body(blk, "sync")
            def _(e):
                p.wait(e, "sync", s_v, s_v.n)
                for d in range(DT):
                    e.dma_start(out=o_x[d * 128:(d + 1) * 128, tsl],
                                in_=XC[:, d * 513 + 1:d * 513 + 513]
                                ).then_inc(s_st.h, 16)
                s_st.n += DT * 16
                e.wait_ge(s_st.h, s_st.n)

        @body(blk, "sync")
        def _(e):
            e.wait_ge(s_st.h, s_st.n)

    return p




def build_A():
    p = Pr("A")
    xg_d = p.dram_in("xg", [D, TL])
    xh_d = p.dram_in("xh", [D, 257])
    wqkv_d = p.dram_in("a_Wqkv", [D, 3 * INNER])
    wo_d = p.dram_in("a_Wo", [INNER, D])
    qs_d = p.dram_in("a_qs", [128, 1])
    ks_d = p.dram_in("a_ks", [128, 1])
    tab_d = {nm: p.dram_in(nm, [128, 2305])
             for nm in ["csq", "ssq", "csk", "ssk"]}
    prot_d = p.dram_in("Prot", [128, 128])
    mask_d = p.dram_in("attn_masks", [17 * 128, 512], BF16)
    id_d = p.dram_in("id128", [128, 128], BF16)
    o_x = p.dram_out("xout", [D, TL])

    with p.ctx:
        WQ = p.sbuf("WQ", [128, DT * 3 * INNER], BF16)
        WOB = p.sbuf("WOB", [128, DT * D], BF16)
        TB = {nm: p.sbuf(nm + "_s", [128, 2305]) for nm in
              ["csq", "ssq", "csk", "ssk"]}
        PROT = p.sbuf("PROT", [128, 128])
        ID = p.sbuf("ID", [128, 128], BF16)
        MK = p.sbuf("MK", [128, 4 * 512], BF16)
        XC = p.sbuf("XC", [128, DT * 513])
        XCB = p.sbuf("XCB", [128, DT * 513], BF16)
        XHB = p.sbuf("XHB", [128, DT * 257])
        XHBB = p.sbuf("XHBB", [128, DT * 257], BF16)
        QB = p.sbuf("QB", [128, DT * TC], BF16)
        KR = p.sbuf("KR", [128, 8 * 768], BF16)
        VR = p.sbuf("VR", [128, TC], BF16)
        VT = p.sbuf("VT", [128, 6 * 1024], BF16)
        AO = p.sbuf("AO", [128, DT * TC], BF16)
        SQ = p.sbuf("SQ", [128, TC])
        PS1 = p.sbuf("PS1", [128, TC])
        QN = p.sbuf("QN", [128, TC])
        T1 = p.sbuf("T1", [128, TC])
        SP = p.sbuf("SP", [128, TC])
        PB = p.sbuf("PB", [128, TC], BF16)
        PT = p.sbuf("PT", [128, TC], BF16)
        rows = p.sbuf("rows", [1, 4 * TC])
        SCL = p.sbuf("SCL", [128, 4])
        qsc = p.sbuf("qsc", [128, 1])
        ksc = p.sbuf("ksc", [128, 1])
        qs8 = p.sbuf("qs8", [128, 1])
        ones_f = p.sbuf("ones_f", [128, 1])
        ones_f1 = p.sbuf("ones_f1", [1, 128])

        psQ = [p.psum(f"psQ{i}", [128, TC]) for i in range(3)]
        psSt = p.psum("psSt", [65, TC])
        psBc = p.psum("psBc", [128, TC])
        psR = p.psum("psR", [128, TC])
        psS = p.psum("psS", [128, TC])
        psX = p.psum("psX", [128, TC], BF16)

        s_ld = p.sem("s_ld")
        s_w = p.sem("s_w")
        s_t = p.sem("s_t")
        s_v = p.sem("s_v")
        s_a = p.sem("s_a")
        s_st = p.sem("s_st")
        blk = p.open_block()
        p._a_relS = 0

        @body(blk, "sync")
        def _(e):
            for nm in ["csq", "ssq", "csk", "ssk"]:
                e.dma_start(out=TB[nm][:], in_=tab_d[nm]).then_inc(s_ld.h, 16)
            e.dma_start(out=PROT[:], in_=prot_d).then_inc(s_ld.h, 16)
            e.dma_start(out=ID[:], in_=id_d).then_inc(s_ld.h, 16)
            e.dma_start(out=qsc[:], in_=qs_d).then_inc(s_ld.h, 16)
            e.dma_start(out=ksc[:], in_=ks_d).then_inc(s_ld.h, 16)
            for d in range(DT):
                e.dma_start(out=XHB[:, d * 257:(d + 1) * 257],
                            in_=xh_d[d * 128:(d + 1) * 128, :]
                            ).then_inc(s_ld.h, 16)
            s_ld.n += (8 + DT) * 16
            e.wait_ge(s_ld.h, s_ld.n)

        @body(blk, "gpsimd")
        def _(e):
            e.dma_start(
                out=bass.AP(WQ, 0, [[DT * 3 * INNER, 128],
                                    [3 * INNER, DT], [1, 3 * INNER]]),
                in_=wqkv_d.rearrange("(k p) o -> p k o", p=128)
            ).then_inc(s_w.h, 16)
            s_w.n += 16
            e.wait_ge(s_w.h, s_w.n)
            e.dma_start(
                out=bass.AP(WOB, 0, [[DT * D, 128], [D, DT], [1, D]]),
                in_=wo_d.rearrange("(k p) o -> p k o", p=128)
            ).then_inc(s_w.h, 16)
            s_w.n += 16
            e.wait_ge(s_w.h, s_w.n)

        @body(blk, "vector")
        def _(e):
            e.memset(ones_f[:], 1.0)
            i = e.memset(ones_f1[:], 1.0)
            inc(i, s_v)
            p.wait(e, "vector", s_ld, s_ld.n)
            i = e.tensor_scalar(qs8[:], qsc[:], 8.0, None, ALU.mult)
            inc(i, s_v)
            i = e.tensor_copy(XHBB[:], XHB[:])
            inc(i, s_v)

        rel = {}

        def emit_qk_evac(o, is_q, N, tab_off, dst_fn):
            ps = psQ[o % 3]
            cs = TB["csq" if is_q else "csk"]
            sn = TB["ssq" if is_q else "ssk"]
            sc = qs8 if is_q else ksc

            @body(blk, "vector")
            def _(e):
                p.wait(e, "vector", s_v, s_v.n)
                p.wait(e, "vector", s_t, s_t.n)
                sg(p, e, "vector", s_v,
                   e.tensor_copy(PS1[:, 0:N], ps[:, 0:N]))
                i = e.tensor_tensor(SQ[:, 0:N], PS1[:, 0:N], PS1[:, 0:N],
                                    ALU.mult)
                inc(i, s_v)
            sqv = s_v.n
            rel[o] = sqv

            @body(blk, "tensor")
            def _(e):
                p.wait(e, "tensor", s_v, sqv)
                i = e.matmul(psSt[0:1, 0:N], ones_f[0:64, :], SQ[0:64, 0:N],
                             start=True, stop=True)
                inc(i, s_t)
                i = e.matmul(psSt[64:65, 0:N], ones_f[64:128, :],
                             SQ[64:128, 0:N], start=True, stop=True)
                inc(i, s_t)
            stt = s_t.n

            @body(blk, "vector")
            def _(e):
                p.wait(e, "vector", s_t, stt)
                sg(p, e, "vector", s_v,
                   e.tensor_scalar(rows[0:1, 0:N], psSt[0:1, 0:N], 1e-24,
                                   None, ALU.max))
                sg(p, e, "vector", s_v,
                   e.tensor_scalar(rows[0:1, TC:TC + N], psSt[64:65, 0:N],
                                   1e-24, None, ALU.max))

            @body(blk, "scalar")
            def _(e):
                p.wait(e, "scalar", s_v, s_v.n)
                r2 = rows[0:1, 2 * TC:2 * TC + N]
                sg(p, e, "scalar", s_a,
                   e.activation(r2, rows[0:1, 0:N], AF.Ln))
                sg(p, e, "scalar", s_a,
                   e.activation(rows[0:1, 0:N], r2, AF.Exp, scale=-0.5))
                sg(p, e, "scalar", s_a,
                   e.activation(r2, rows[0:1, TC:TC + N], AF.Ln))
                sg(p, e, "scalar", s_a,
                   e.activation(rows[0:1, TC:TC + N], r2, AF.Exp,
                                scale=-0.5))
            aat = s_a.n

            @body(blk, "tensor")
            def _(e):
                p.wait(e, "tensor", s_a, aat)
                i = e.matmul(psBc[:, 0:N], ones_f1[:], rows[0:1, 0:N],
                             start=True, stop=True)
                inc(i, s_t)
                i = e.matmul(psR[:, 0:N], ones_f1[:], rows[0:1, TC:TC + N],
                             start=True, stop=True)
                inc(i, s_t)
            bct = s_t.n

            @body(blk, "vector")
            def _(e):
                p.wait(e, "vector", s_t, bct)
                i = e.scalar_tensor_tensor(QN[0:64, 0:N], PS1[0:64, 0:N],
                                           sc[0:64, :], psBc[0:64, 0:N],
                                           ALU.mult, ALU.mult)
                inc(i, s_v)
                i = e.scalar_tensor_tensor(QN[64:128, 0:N], PS1[64:128, 0:N],
                                           sc[64:128, :], psR[64:128, 0:N],
                                           ALU.mult, ALU.mult)
                inc(i, s_v)
                p.wait(e, "vector", s_v, s_v.n)
            qnv = s_v.n

            @body(blk, "tensor")
            def _(e):
                p.wait(e, "tensor", s_v, qnv)
                i = e.matmul(psR[:, 0:N], PROT[:], QN[:, 0:N],
                             start=True, stop=True)
                inc(i, s_t)
            rott = s_t.n

            @body(blk, "vector")
            def _(e):
                sg(p, e, "vector", s_v,
                   e.tensor_tensor(T1[:, 0:N], QN[:, 0:N],
                                   cs[:, tab_off:tab_off + N], ALU.mult))
                p.wait(e, "vector", s_t, rott)
                sg(p, e, "vector", s_v,
                   e.tensor_tensor(SQ[:, 0:N], psR[:, 0:N],
                                   sn[:, tab_off:tab_off + N], ALU.mult))
                i = e.tensor_tensor(dst_fn(), T1[:, 0:N], SQ[:, 0:N],
                                    ALU.add)
                inc(i, s_v)

        def emit_v_evac(o, N, ring0):
            ps = psQ[o % 3]
            ph = o - 16

            @body(blk, "vector")
            def _(e):
                p.wait(e, "vector", s_v, s_v.n)
                p.wait(e, "vector", s_t, s_t.n)
                i = e.tensor_copy(VR[:, 0:N], ps[:, 0:N])
                inc(i, s_v)
            vrv = s_v.n
            rel[o] = vrv
            ntile = N // 128

            @body(blk, "tensor")
            def _(e):
                p.wait(e, "tensor", s_v, vrv)
                for t in range(ntile):
                    i = e.matmul(psX[:, t * 128:(t + 1) * 128],
                                 VR[:, t * 128:(t + 1) * 128], ID[:],
                                 is_transpose=True, start=True, stop=True)
                    inc(i, s_t)
            vtt = s_t.n

            @body(blk, "vector")
            def _(e):
                p.wait(e, "vector", s_t, vtt)
                i = e.tensor_copy(
                    bass.AP(VT, ring0 * 1024 + ph * 128,
                            [[6 * 1024, 128], [1024, ntile], [1, 128]]),
                    bass.AP(psX, 0, [[TC, 128], [128, ntile], [1, 128]]))
                inc(i, s_v)

        def xh_rhs(kd, N=256):
            if kd < 4:
                return XHBB[:, kd * 257 + 1:kd * 257 + 1 + N]
            return XHBB[:, kd * 257:kd * 257 + N]

        for o in range(8, 24):
            @body(blk, "tensor")
            def _(e, o=o):
                if o == 8:
                    p.wait(e, "tensor", s_w, 16)
                    p.wait(e, "tensor", s_ld, s_ld.n)
                    p.wait(e, "tensor", s_v, 3)
                if o >= 11:
                    p.wait(e, "tensor", s_v, rel[o - 3])
                for k in range(DT):
                    i = e.matmul(psQ[o % 3][:, 0:256],
                                 WQ[:, k * 3 * INNER + o * 128:
                                    k * 3 * INNER + o * 128 + 128],
                                 xh_rhs(k), start=(k == 0), stop=(k == DT - 1))
                inc(i, s_t)

            if o < 16:
                ph = o - 8
                emit_qk_evac(
                    o, False, 256, 1,
                    lambda ph=ph: KR[:, ph * 768:ph * 768 + 256])
            else:
                emit_v_evac(o, 256, 0)

        for tc in range(NT):
            tsl = slice(tc * TC, (tc + 1) * TC)

            @body(blk, "sync")
            def _(e, tc=tc):
                p.wait(e, "sync", s_v, s_v.n)
                p.wait(e, "sync", s_t, s_t.n)
                p.wait(e, "sync", s_st, s_st.n)
                for d in range(DT):
                    if tc == 0:
                        e.dma_start(out=XC[:, d * 513:d * 513 + 1],
                                    in_=xh_d[d * 128:(d + 1) * 128, 256:257],
                                    allow_slow_non_contiguous=True
                                    ).then_inc(s_ld.h, 16)
                        e.dma_start(out=XC[:, d * 513 + 1:d * 513 + 513],
                                    in_=xg_d[d * 128:(d + 1) * 128, 0:TC]
                                    ).then_inc(s_ld.h, 16)
                        s_ld.n += 32
                    else:
                        e.dma_start(
                            out=XC[:, d * 513:d * 513 + 513],
                            in_=xg_d[d * 128:(d + 1) * 128,
                                     tc * TC - 1:(tc + 1) * TC]
                            ).then_inc(s_ld.h, 16)
                        s_ld.n += 16
                for j in range(4):
                    gi = tc * 4 + j + 1
                    e.dma_start(out=MK[:, j * 512:(j + 1) * 512],
                                in_=mask_d[gi * 128:(gi + 1) * 128, :]
                                ).then_inc(s_ld.h, 16)
                s_ld.n += 4 * 16
                e.wait_ge(s_ld.h, s_ld.n)

            @body(blk, "vector")
            def _(e, tc=tc):
                p.wait(e, "vector", s_t, s_t.n)
                p.wait(e, "vector", s_v, s_v.n)
                p.wait(e, "vector", s_ld, s_ld.n)
                for d in range(DT):
                    i = e.tensor_copy(XCB[:, d * 513:(d + 1) * 513],
                                      XC[:, d * 513:(d + 1) * 513])
                    inc(i, s_v)
                if tc > 0:
                    sg(p, e, "vector", s_v, e.tensor_copy(
                        bass.AP(KR, 0, [[8 * 768, 128], [768, 8], [1, 256]]),
                        bass.AP(KR, 512, [[8 * 768, 128], [768, 8],
                                          [1, 256]])))
                    sg(p, e, "vector", s_v,
                       e.tensor_copy(VT[:, 0:2048], VT[:, 4096:6144]))
                else:
                    i = e.memset(SCL[:, 3:4], 0.0)
                    inc(i, s_v)
            shv = s_v.n

            tab0 = 257 + tc * TC

            def xc_rhs(kd):
                if kd < 4:
                    return XCB[:, kd * 513 + 1:kd * 513 + 513]
                return XCB[:, kd * 513:kd * 513 + 512]

            for o in range(24):
                @body(blk, "tensor")
                def _(e, o=o, shv=shv):
                    if o == 0:
                        p.wait(e, "tensor", s_v, shv)
                        p.wait(e, "tensor", s_w, 32)
                        p.wait(e, "tensor", s_ld, s_ld.n)
                    if o >= 3:
                        p.wait(e, "tensor", s_v, rel[o - 3])
                    for k in range(DT):
                        i = e.matmul(psQ[o % 3][:, :],
                                     WQ[:, k * 3 * INNER + o * 128:
                                        k * 3 * INNER + o * 128 + 128],
                                     xc_rhs(k), start=(k == 0),
                                     stop=(k == DT - 1))
                    inc(i, s_t)

                if o < 8:
                    emit_qk_evac(
                        o, True, TC, tab0,
                        lambda o=o: QB[:, o * TC:(o + 1) * TC])
                elif o < 16:
                    ph = o - 8
                    emit_qk_evac(
                        o, False, TC, tab0,
                        lambda ph=ph: KR[:, ph * 768 + 256:ph * 768 + 768])
                else:
                    emit_v_evac(o, TC, 2)

            for j in range(4):
                rc0 = (j // 2) * 256
                for ph in range(DT):
                    for hh in range(2):
                        rsl = slice(hh * 64, (hh + 1) * 64)

                        @body(blk, "tensor")
                        def _(e, j=j, ph=ph, hh=hh, rc0=rc0):
                            p.wait(e, "tensor", s_v, p._a_relS)
                            i = e.matmul(
                                psS[:, :],
                                QB[rsl, ph * TC + j * 128:
                                   ph * TC + (j + 1) * 128],
                                KR[rsl, ph * 768 + rc0:ph * 768 + rc0 + 512],
                                start=True, stop=False)
                            i = e.matmul(psS[:, :], ID[:],
                                         MK[:, j * 512:(j + 1) * 512],
                                         start=False, stop=True)
                            inc(i, s_t)
                        qkt = s_t.n

                        @body(blk, "vector")
                        def _(e, qkt=qkt):
                            p.wait(e, "vector", s_t, qkt)
                            i = e.tensor_reduce(SCL[:, 0:1], psS[:, :],
                                                axis=AX.X, op=ALU.max,
                                                negate=True)
                            inc(i, s_v)
                        mxv = s_v.n

                        @body(blk, "scalar")
                        def _(e, mxv=mxv):
                            p.wait(e, "scalar", s_v, mxv)
                            i = e.activation(SP[:, :], psS[:, :], AF.Exp,
                                             bias=SCL[:, 0:1],
                                             accum_out=SCL[:, 1:2])
                            inc(i, s_a)
                        spa = s_a.n

                        @body(blk, "vector")
                        def _(e, spa=spa):
                            p.wait(e, "vector", s_a, spa)
                            sg(p, e, "vector", s_v,
                               e.reciprocal(SCL[:, 2:3], SCL[:, 1:2]))
                            i = e.tensor_scalar(PB[:, :], SP[:, :],
                                                SCL[:, 2:3], None, ALU.mult)
                            inc(i, s_v)
                        pbv = s_v.n

                        @body(blk, "tensor")
                        def _(e, pbv=pbv):
                            p.wait(e, "tensor", s_v, pbv)
                            for t in range(4):
                                i = e.matmul(psX[:, t * 128:(t + 1) * 128],
                                             PB[:, t * 128:(t + 1) * 128],
                                             ID[:], is_transpose=True,
                                             start=True, stop=True)
                                inc(i, s_t)
                        ptt = s_t.n

                        @body(blk, "vector")
                        def _(e, ptt=ptt):
                            p.wait(e, "vector", s_t, ptt)
                            i = e.tensor_copy(PT[:, :], psX[:, :])
                            inc(i, s_v)
                        ptv = s_v.n

                        @body(blk, "tensor")
                        def _(e, j=j, ph=ph, hh=hh, ptv=ptv, spa=spa):
                            p.wait(e, "tensor", s_v, ptv)
                            p.wait(e, "tensor", s_a, spa)
                            for t in range(4):
                                jt = (j // 2) * 2 + t
                                i = e.matmul(
                                    psS[0:64, 0:128],
                                    VT[:, jt * 1024 + ph * 128 + hh * 64:
                                       jt * 1024 + ph * 128 + hh * 64 + 64],
                                    PT[:, t * 128:(t + 1) * 128],
                                    start=(t == 0), stop=(t == 3))
                            inc(i, s_t)
                        avt = s_t.n

                        @body(blk, "vector")
                        def _(e, j=j, ph=ph, hh=hh, avt=avt):
                            p.wait(e, "vector", s_t, avt)
                            i = e.tensor_copy(
                                AO[hh * 64:(hh + 1) * 64,
                                   ph * TC + j * 128:ph * TC + (j + 1) * 128],
                                psS[0:64, 0:128])
                            inc(i, s_v)
                            p._a_relS = s_v.n

            def evac_o(e, o, ps):
                x_o = XC[:, o * 513 + 1:o * 513 + 513]
                return e.tensor_tensor(x_o, x_o, ps[:, :], ALU.add)

            mm(p, s_t, psQ, DT, DT,
               lambda k, o: WOB[:, k * D + o * 128:k * D + o * 128 + 128],
               lambda k: AO[:, k * TC:(k + 1) * TC],
               [("vector", s_v, evac_o)],
               tdeps=[(s_v, s_v.n), (s_a, s_a.n), (s_w, 32)])

            @body(blk, "sync")
            def _(e):
                p.wait(e, "sync", s_v, s_v.n)
                for d in range(DT):
                    e.dma_start(out=o_x[d * 128:(d + 1) * 128, tsl],
                                in_=XC[:, d * 513 + 1:d * 513 + 513]
                                ).then_inc(s_st.h, 16)
                s_st.n += DT * 16
                e.wait_ge(s_st.h, s_st.n)

        @body(blk, "sync")
        def _(e):
            e.wait_ge(s_st.h, s_st.n)

    return p




_BUILDERS = {"L01a": build_L01a, "L01b": build_L01b,
             "L02": build_L02,
             "MA0": lambda: build_MA(0),
             "MA1": lambda: build_MA(1),
             "MB0": lambda: build_MB(0),
             "MB1": lambda: build_MB(1),
             "G": lambda: build_G(False),
             "G5": lambda: build_G(True),
             "A": lambda: build_A()}
for _u in range(1, 5):
    _BUILDERS[f"L01b.{_u}"] = (lambda u: (lambda: build_L01b(u)))(_u)
for _u in range(1, 7):
    _BUILDERS[f"L01a.{_u}"] = (lambda u: (lambda: build_L01a(u)))(_u)
_CACHE = {}


def _run(name, sh, pc, env):
    if name not in _CACHE:
        _CACHE[name] = _BUILDERS[name]()
    p = _CACHE[name]
    in_maps = []
    for c in range(NCORES):
        m = {}
        for n in p.in_names:
            if n in env[c]:
                m[n] = env[c][n]
            elif n in pc[c]:
                m[n] = pc[c][n]
            else:
                m[n] = sh[n]
        in_maps.append(m)
    res = run_bass_kernel_spmd(p.nc, in_maps, core_ids=list(range(NCORES)))
    for c in range(NCORES):
        for n in p.out_names:
            env[c][n] = res.results[c][n]
    return res


def _bd_rglru(env):
    for c in range(NCORES):
        b, q = divmod(c, 4)
        hend = np.zeros((128, 3 * DT), np.float32)
        pend = np.ones((128, 3 * DT), np.float32)
        for j in range(3):
            src_c = b * 4 + j
            he = np.asarray(env[src_c]["hs_loc"], np.float32)[:, -1]
            pe = np.asarray(env[src_c]["P_loc"], np.float32)[:, -1]
            hend[:, j * DT:(j + 1) * DT] = he.reshape(DT, 128).T
            pend[:, j * DT:(j + 1) * DT] = pe.reshape(DT, 128).T
        env[c]["hend_g"] = np.ascontiguousarray(hend)
        env[c]["pend_g"] = np.ascontiguousarray(pend)


def _bd_xi(env, key):
    for c in range(NCORES):
        b, q = divmod(c, 4)
        env[c]["xi"] = env[c][key]
        if q == 0:
            env[c]["xi_tail"] = np.zeros((ED, 4), np.float32)
        else:
            env[c]["xi_tail"] = np.ascontiguousarray(
                np.asarray(env[c - 1][key], np.float32)[:, TL - 4:TL])


def _bd_mamba(env):
    for c in range(NCORES):
        b, q = divmod(c, 4)
        hg = np.zeros((128, 3 * EDT * NST), np.float32)
        sg_ = np.zeros((128, 3 * EDT), np.float32)
        for j in range(3):
            src = b * 4 + j
            hg[:, j * EDT * NST:(j + 1) * EDT * NST] = \
                np.asarray(env[src]["hend_m"], np.float32)
            sg_[:, j * EDT:(j + 1) * EDT] = \
                np.asarray(env[src]["sumd"], np.float32)
        env[c]["hendm_g"] = np.ascontiguousarray(hg)
        env[c]["sumd_g"] = np.ascontiguousarray(sg_)


def kernel(**inputs):
    sh, pc = _prep(inputs)
    env = [dict() for _ in range(NCORES)]
    _run("L01a", sh, pc, env)
    _run("L01b", sh, pc, env)
    _bd_rglru(env)
    _run("L02", sh, pc, env)
    _bd_xi(env, "xi0")
    _run("MA0", sh, pc, env)
    _bd_mamba(env)
    for c in range(NCORES):
        env[c]["zin"] = env[c]["z0"]
        env[c]["xbase"] = env[c]["xn1"]
    _run("MB0", sh, pc, env)
    _bd_xi(env, "xi1")
    _run("MA1", sh, pc, env)
    _bd_mamba(env)
    for c in range(NCORES):
        env[c]["zin"] = env[c]["z1"]
        env[c]["xbase"] = env[c]["hnext"]
    _run("MB1", sh, pc, env)

    for c in range(NCORES):
        env[c]["xf"] = env[c]["xn1"]
    for l in range(6):
        for c in range(NCORES):
            b, q = divmod(c, 4)
            env[c]["xg"] = env[c]["xf"]
            env[c]["xh"] = np.zeros((D, 257), np.float32) if q == 0 else \
                np.ascontiguousarray(
                    np.asarray(env[c - 1]["xf"], np.float32)[:, TL - 257:TL])
            env[c]["a_Wqkv"] = sh[f"f{l}_Wqkv"]
            env[c]["a_Wo"] = sh[f"f{l}_Wo"]
            env[c]["a_qs"] = sh[f"f{l}_qs"]
            env[c]["a_ks"] = sh[f"f{l}_ks"]
        _run("A", sh, pc, env)
        for c in range(NCORES):
            env[c]["xa"] = env[c]["xout"]
        for c in range(NCORES):
            b, q = divmod(c, 4)
            env[c]["xg"] = env[c]["xa"]
            env[c]["xgh"] = np.zeros((D, 1), np.float32) if q == 0 else \
                np.ascontiguousarray(
                    np.asarray(env[c - 1]["xa"], np.float32)[:, TL - 1:TL])
            for nm in ["W1a", "W1g", "W2", "b1a", "b1g", "b2", "lng", "lnb"]:
                env[c]["g_" + nm] = sh[f"f{l}_{nm}"]
            if l == 5:
                env[c]["xpf"] = env[c]["xp"]
        _run("G5" if l == 5 else "G", sh, pc, env)
        for c in range(NCORES):
            env[c]["xf"] = env[c]["xout"]

    out = np.empty((2, 8192, D), np.float32)
    for c in range(NCORES):
        b, q = divmod(c, 4)
        out[b, q * TL:(q + 1) * TL, :] = \
            np.asarray(env[c]["xf"], np.float32).T
    return out
